# version 23
# baseline (speedup 1.0000x reference)
import math
import numpy as np
import concourse.bass as bass
import concourse.mybir as mybir
from concourse.bass_utils import run_bass_kernel_spmd

F32 = mybir.dt.float32
F16 = mybir.dt.float16
BF16 = mybir.dt.bfloat16
I32 = mybir.dt.int32
U8 = mybir.dt.uint8
AF = mybir.ActivationFunctionType
ALU = mybir.AluOpType
AX = mybir.AxisListType

D = 2048
T = 2048
NH = 16
DH = 128
DC = 16
NCORES = 8
ISQ = 1.0 / math.sqrt(DH)
EPS_LN = 1e-5
EPS_Q = 1e-7
CC_BOUND = (1.03 * (1.0 / math.sqrt(128))) ** 2
CLN_HALF = 0.5 * math.log(CC_BOUND)

PHASES = 3
DEBUG = False

ENGINES = ("sync", "gpsimd", "scalar", "vector", "tensor")
DMA_SEMS = ("ds", "dg", "sq0", "sq1", "so0", "so1", "dcn",
            "dwq0", "dwq1", "dwg0", "dwg1", "dws0", "dws1",
            "dwv0", "dwv1", "dwv2", "dwv3",
            "dq0", "dq1", "dxt0", "dxt1", "dxt2", "dxt3",
            "doc0", "doc1", "dwo0", "dwo1", "dff", "dwb", "dxg0", "dxg1")
ALL_SEMS = DMA_SEMS + ("cs", "cv", "ct")


class Gen:

    def __init__(self):
        self.ops = {e: [] for e in ENGINES}
        self.c = {s: 0 for s in ALL_SEMS}
        self.sems = {}
        self.last_wait = {(e, s): 0 for e in ENGINES for s in ALL_SEMS}

    def do(self, eng, fn, inc=None):
        if inc is None:
            self.ops[eng].append((fn, None, 0))
        else:
            n = 16 if inc in DMA_SEMS else 1
            self.c[inc] += n
            self.ops[eng].append((fn, inc, n))
        return self.c.get(inc, 0)

    def wait(self, eng, sem, val=None):
        v = self.c[sem] if val is None else val
        if v <= 0 or self.last_wait[(eng, sem)] >= v:
            return
        self.last_wait[(eng, sem)] = v
        self.ops[eng].append((lambda e, s=sem, vv=v: e.wait_ge(self.sems[s], vv),
                              None, 0))

    def fence(self, eng, sem):
        self.wait(eng, sem, self.c[sem])

    def run(self, eng_name, e):
        for fn, inc, n in self.ops[eng_name]:
            ins = fn(e)
            if inc is not None:
                ins.then_inc(self.sems[inc], n)


class SBAlloc:

    def __init__(self, nc, backing, base_addr, total):
        self.nc = nc
        self.backing = backing
        self.base = base_addr
        self.total = total
        self.free_list = [(0, total)]
        self.live = {}

    def alloc(self, name, shape, dtype):
        per_part = int(np.prod(shape[1:])) * mybir.dt.size(dtype)
        per_part = (per_part + 31) // 32 * 32
        for i, (s, sz) in enumerate(self.free_list):
            if sz >= per_part:
                if sz == per_part:
                    del self.free_list[i]
                else:
                    self.free_list[i] = (s + per_part, sz - per_part)
                h = self.nc.virtual_sbuf_tensor(name, list(shape), dtype)
                self.nc._tensor(h.name, h.shape, h.dtype, type="SB")
                mloc = self.nc.lookup_mloc(h)
                mloc.allocated = True
                mloc.addr = self.base + s
                self.live[name] = (s, per_part)
                return h
        raise MemoryError(f"SBAlloc: no room for {name} ({per_part} B); "
                          f"free={self.free_list}")

    def free(self, *names):
        for name in names:
            s, sz = self.live.pop(name)
            self.free_list.append((s, sz))
        self.free_list.sort()
        out = []
        for seg in self.free_list:
            if out and out[-1][0] + out[-1][1] == seg[0]:
                out[-1] = (out[-1][0], out[-1][1] + seg[1])
            else:
                out.append(seg)
        self.free_list = out


def build_program():
    nc = bass.Bass("TRN2", target_bir_lowering=False, debug=False)

    xT = nc.dram_tensor("xT", [D, T], F32, kind="ExternalInput")
    x_tm = nc.dram_tensor("x_tm", [T, D], F32, kind="ExternalInput")
    wqk_t = nc.dram_tensor("wqk_t", [32, 128, 2048], F32, kind="ExternalInput")
    wv_r = nc.dram_tensor("wv_r", [D, D], F32, kind="ExternalInput")
    wo_r = nc.dram_tensor("wo_r", [D, D], F32, kind="ExternalInput")
    wffn_r = nc.dram_tensor("wffn_r", [D, D], F16, kind="ExternalInput")
    ropec = nc.dram_tensor("ropec", [128, T], F32, kind="ExternalInput")
    ropes = nc.dram_tensor("ropes", [128, T], F32, kind="ExternalInput")
    tri_h = nc.dram_tensor("tri_h", [128, 128], F32, kind="ExternalInput")
    rsw_h = nc.dram_tensor("rsw_h", [128, 128], F32, kind="ExternalInput")
    idn_h = nc.dram_tensor("idn_h", [128, 128], F32, kind="ExternalInput")
    wsc_h = nc.dram_tensor("wsc_h", [128, D], F32, kind="ExternalInput")
    bia_h = nc.dram_tensor("bia_h", [128, D], F32, kind="ExternalInput")
    out = nc.dram_tensor("out", [T, D], F32, kind="ExternalOutput")

    scratch_kind = "ExternalOutput" if DEBUG else "Internal"
    qsp = nc.dram_tensor("qsp", [NH, 128, T], F16, kind=scratch_kind)
    osp = nc.dram_tensor("osp", [16, 128, NH * 128], F16, kind=scratch_kind)
    if DEBUG:
        dbg_k = nc.dram_tensor("dbg_k", [128, NH * T], F16,
                               kind="ExternalOutput")
        dbg_v = nc.dram_tensor("dbg_v", [128, 64 * 512], BF16,
                               kind="ExternalOutput")
        dbg_mb = nc.dram_tensor("dbg_mb", [128, 16], F32,
                                kind="ExternalOutput")
        dbg_r = nc.dram_tensor("dbg_r", [128, 32], F32, kind="ExternalOutput")
        dbg_p = nc.dram_tensor("dbg_p", [128, 16], F32, kind="ExternalOutput")

    g = Gen()
    from contextlib import ExitStack
    with ExitStack() as stack:
        for s in ALL_SEMS:
            g.sems[s] = stack.enter_context(nc.semaphore(s))
        block = stack.enter_context(nc.Block())

        ARENA = 212832
        backing = stack.enter_context(nc.sbuf_tensor("arena", [128, ARENA], U8))
        base_addr = nc.lookup_mloc(backing).addr
        sb = SBAlloc(nc, backing, base_addr, ARENA)

        pT = [stack.enter_context(nc.psum_tensor(f"pp{i}", [128, 1024], F32))
              for i in range(4)]
        pband = [pT[i // 2][:, (i % 2) * 512:(i % 2) * 512 + 512]
                 for i in range(8)]

        bank_free = {}

        def wait_bank(eng, b):
            if b in bank_free:
                sem, val = bank_free[b]
                g.wait(eng, sem, val)

        def wait_banks(eng, bs):
            for b in bs:
                wait_bank(eng, b)

        def set_banks(bs, cond):
            for b in bs:
                bank_free[b] = cond

        freed = {}

        def wait_buf(eng, key):
            for sem, val in freed.get(key, ()):
                g.wait(eng, sem, val)

        def set_free(key, *conds):
            freed[key] = conds

        cos_h = sb.alloc("cos_h", [128, T], F16)
        sin_h = sb.alloc("sin_h", [128, T], F16)
        tri_bf = sb.alloc("tri_bf", [128, 128], BF16)
        r16 = sb.alloc("r16", [128, 128], F16)
        ones_bf = sb.alloc("ones_bf", [128, 128], BF16)
        eps_t = sb.alloc("eps_t", [128, 1], F32)
        cln_t = sb.alloc("cln_t", [128, 1], F32)
        mbias = sb.alloc("mbias", [128, 16], F32)

        x_h = sb.alloc("x_h", [128, DC * 1024], F16)
        k_all = sb.alloc("k_all", [128, NH * T], F16)
        v_flat = sb.alloc("v_flat", [128, 64 * 512], BF16)
        wqk_f = [sb.alloc(f"wqk_f{i}", [128, 2048], F32) for i in range(2)]
        wqk_h = [sb.alloc(f"wqk_h{i}", [128, 2048], F16) for i in range(2)]
        wvf = [wqk_f[0][:, j * 512:(j + 1) * 512] for j in range(4)]
        wvh = [wqk_h[0][:, j * 512:(j + 1) * 512] for j in range(4)]
        xsf = [wqk_f[0][:, 0:1024], wqk_f[0][:, 1024:2048],
               wqk_f[1][:, 0:1024], wqk_f[1][:, 1024:2048]]
        xsf2 = [wqk_f[1][:, 0:1024], wqk_f[1][:, 1024:2048]]
        g16 = [sb.alloc(f"g16_{i}", [128, 1024], F16) for i in range(2)]
        qs16 = [sb.alloc(f"qs16_{i}", [128, 1024], F16) for i in range(2)]
        sqbuf = sb.alloc("sqbuf", [128, 1024], BF16)
        qo = [sb.alloc(f"qo{i}", [128, 1024], F16) for i in range(2)]
        normmax = sb.alloc("normmax", [128, 64], F32)
        nmax32 = sb.alloc("nmax32", [128, 32], F32)
        prod = sb.alloc("prod", [128, 16], F32)

        g.do("sync", lambda e: e.dma_start(wqk_f[0][:], ropec[:]), inc="dcn")
        g.do("sync", lambda e: e.dma_start(wqk_f[1][:], ropes[:]), inc="dcn")
        g.wait("vector", "dcn")
        g.do("vector", lambda e: e.tensor_copy(cos_h[:], wqk_f[0][:]), inc="cv")
        sin_cv = g.do("vector", lambda e: e.tensor_copy(sin_h[:], wqk_f[1][:]),
                      inc="cv")
        g.wait("sync", "cv", sin_cv)
        g.do("sync", lambda e: e.dma_start(wqk_f[0][:, 0:128], tri_h[:]),
             inc="dcn")
        g.do("sync", lambda e: e.dma_start(wqk_f[0][:, 128:256], rsw_h[:]),
             inc="dcn")
        g.wait("vector", "dcn")
        g.do("vector", lambda e: e.tensor_copy(tri_bf[:], wqk_f[0][:, 0:128]),
             inc="cv")
        g.do("vector", lambda e: e.tensor_copy(r16[:], wqk_f[0][:, 128:256]),
             inc="cv")
        g.do("vector", lambda e: e.memset(ones_bf[:], 1.0), inc="cv")
        g.do("vector", lambda e: e.memset(eps_t[:], EPS_LN), inc="cv")
        g.do("vector", lambda e: e.memset(cln_t[:], CLN_HALF), inc="cv")
        g.do("vector", lambda e: e.memset(normmax[:], 0.0), inc="cv")
        cv_consts = g.c["cv"]
        g.wait("tensor", "cv", cv_consts)
        g.wait("scalar", "cv", cv_consts)
        g.fence("vector", "cv")

        for d in range(DC):
            sp = d % 4
            if d == 0:
                g.wait("sync", "cv", cv_consts)
            wait_buf("sync", ("xs", sp))
            g.do("sync", lambda e, d=d, sp=sp: e.dma_start(
                xsf[sp][:], xT[d * 128:(d + 1) * 128, 0:1024]), inc=f"dxt{sp}")
            g.wait("vector", f"dxt{sp}")
            xc = g.do("vector", lambda e, d=d, sp=sp: e.tensor_copy(
                x_h[:, d * 1024:(d + 1) * 1024], xsf[sp][:]), inc="cv")
            set_free(("xs", sp), ("cv", xc))
        x_ready = g.c["cv"]
        for pp_ in range(2):
            set_free(("wqk_f", pp_), ("cv", x_ready))
            set_free(("wqk_f0h", pp_), ("cv", x_ready))
            set_free(("wqk_f1h", pp_), ("cv", x_ready))

        qk_par = 0
        ch_i = 0

        def emit_chain_pre(st):
            mainT = pT[st["par"]]
            mi = st["par"] * 2
            gi = st["gi"]
            g.wait("scalar", "ct", st["mm"])
            wait_buf("scalar", ("g16", gi))
            st["gready"] = g.do("scalar", lambda e, mainT=mainT, gi=gi:
                                e.activation(
                g16[gi][:], mainT[:], AF.Copy), inc="cs")
            wait_buf("scalar", ("sqb",))
            st["sqd"] = g.do("scalar", lambda e, mainT=mainT: e.activation(
                sqbuf[:], mainT[:], AF.Square), inc="cs")
            set_banks([mi, mi + 1], ("cs", st["sqd"]))

        def emit_chain_mid(st):
            o, t0h, gi = st["o"], st["t0h"], st["gi"]
            wait_banks("tensor", [4, 5])
            g.wait("tensor", "cs", st["gready"])
            g.do("tensor", lambda e, gi=gi: e.matmul(
                pband[4][:], r16[:], g16[gi][:, 0:512], start=True, stop=True),
                inc="ct")
            st["rmm"] = g.do("tensor", lambda e, gi=gi: e.matmul(
                pband[5][:], r16[:], g16[gi][:, 512:1024], start=True,
                stop=True), inc="ct")
            wait_banks("tensor", [6, 7])
            g.wait("tensor", "cs", st["sqd"])
            g.do("tensor", lambda e: e.matmul(
                pband[6][:], ones_bf[:], sqbuf[:, 0:512], start=True,
                stop=True), inc="ct")
            st["nmm"] = g.do("tensor", lambda e: e.matmul(
                pband[7][:], ones_bf[:], sqbuf[:, 512:1024], start=True,
                stop=True), inc="ct")
            set_free(("sqb",), ("ct", st["nmm"]))
            if o < 16:
                dst = qo[gi][:]
                wait_buf("vector", ("qo", gi))
            else:
                dst = k_all[:, (o - 16) * T + t0h:(o - 16) * T + t0h + 1024]
            st["dst"] = dst
            g.wait("vector", "cs", st["gready"])
            va = g.do("vector", lambda e, dst=dst, gi=gi, t0h=t0h:
                      e.tensor_tensor(
                dst, g16[gi][:], cos_h[:, t0h:t0h + 1024], ALU.mult),
                inc="cv")
            set_free(("g16", gi), ("cv", va), ("ct", st["rmm"]))

        def emit_chain_tail(st):
            o, t0h, gi, dst = st["o"], st["t0h"], st["gi"], st["dst"]
            g.wait("scalar", "ct", st["rmm"])
            wait_buf("scalar", ("qs16", gi))
            qsr = g.do("scalar", lambda e, gi=gi: e.activation(
                qs16[gi][:], pT[2][:], AF.Copy), inc="cs")
            set_banks([4, 5], ("cs", qsr))
            g.wait("vector", "cs", qsr)
            g.do("vector", lambda e, gi=gi, t0h=t0h: e.tensor_tensor(
                qs16[gi][:], qs16[gi][:], sin_h[:, t0h:t0h + 1024], ALU.mult),
                inc="cv")
            g.fence("vector", "cv")
            qdone = g.do("vector", lambda e, dst=dst, gi=gi: e.tensor_tensor(
                dst, dst, qs16[gi][:], ALU.add), inc="cv")
            set_free(("qs16", gi), ("cv", qdone))
            if o < 16:
                g.wait("scalar", "cv", qdone)
                qsd = g.do("scalar", lambda e, o=o, t0h=t0h, gi=gi: e.dma_start(
                    qsp[o, :, t0h:t0h + 1024], qo[gi][:]), inc=f"sq{gi}")
                set_free(("qo", gi), (f"sq{gi}", qsd))
            col = o * 2 + (t0h // 1024)
            g.wait("vector", "ct", st["nmm"])
            g.do("vector", lambda e, col=col: e.tensor_reduce(
                normmax[:, col:col + 1], pT[3][:], AX.X, ALU.max), inc="cv")
            set_banks([6, 7], ("cv", g.c["cv"]))

        x_ready_h1 = 0
        for Hh in range(2):
            t0h = Hh * 1024
            x_ready_cv = x_ready if Hh == 0 else x_ready_h1

            chain1 = None
            chain2 = None
            for o in range(34):
                nxt = None
                if o < 32:
                    p = o % 2
                    wait_buf("sync", ("wqk_f0h", p))
                    d0 = g.do("sync", lambda e, o=o, p=p: e.dma_start(
                        wqk_f[p][:, 0:1024], wqk_t[o, :, 0:1024]),
                        inc=f"dwq{p}")
                    wait_buf("gpsimd", ("wqk_f1h", p))
                    g.do("gpsimd", lambda e, o=o, p=p: e.dma_start(
                        wqk_f[p][:, 1024:2048], wqk_t[o, :, 1024:2048]),
                        inc=f"dwg{p}")
                if chain1 is not None:
                    emit_chain_pre(chain1)
                if o < 32:
                    g.wait("vector", f"dwq{p}", d0)
                    wait_buf("vector", ("wqk_h", p))
                    wc0 = g.do("vector", lambda e, p=p: e.tensor_copy(
                        wqk_h[p][:, 0:1024], wqk_f[p][:, 0:1024]), inc="cv")
                    set_free(("wqk_f0h", p), ("cv", wc0))
                    g.wait("scalar", f"dwg{p}")
                    wait_buf("scalar", ("wqk_h", p))
                    wcast = g.do("scalar", lambda e, p=p: e.activation(
                        wqk_h[p][:, 1024:2048], wqk_f[p][:, 1024:2048],
                        AF.Copy), inc="cs")
                    set_free(("wqk_f1h", p), ("cs", wcast))
                    set_free(("wqk_f", p), ("cv", wc0), ("cs", wcast))
                if chain2 is not None:
                    emit_chain_tail(chain2)
                    chain2 = None
                if o < 32:
                    g.wait("tensor", "cv", max(wc0, x_ready_cv))
                    mainT = pT[qk_par]
                    wait_banks("tensor", [qk_par * 2, qk_par * 2 + 1])
                    mm_done = 0
                    for d in range(DC):
                        if d == 8:
                            g.wait("tensor", "cs", wcast)
                        for tb in range(2):
                            mm_done = g.do("tensor", lambda e, p=p, d=d, tb=tb,
                                           mainT=mainT: e.matmul(
                                mainT[:, tb * 512:tb * 512 + 512],
                                wqk_h[p][:, d * 128:(d + 1) * 128],
                                x_h[:, d * 1024 + tb * 512:
                                    d * 1024 + tb * 512 + 512],
                                start=(d == 0), stop=(d == DC - 1)),
                                inc="ct" if (d == DC - 1 and tb == 1) else None)
                    set_free(("wqk_h", p), ("ct", mm_done))
                    nxt = {"o": o, "par": qk_par, "mm": mm_done, "t0h": t0h,
                           "gi": ch_i % 2}
                    ch_i += 1
                    qk_par ^= 1
                if chain1 is not None:
                    emit_chain_mid(chain1)
                    chain2 = chain1
                chain1 = nxt

            for og in range(4):
                d_mm = {}
                for d in range(DC):
                    sp = d % 4
                    q_eng = "sync" if sp % 2 == 0 else "gpsimd"
                    wait_buf(q_eng, ("wvf", sp))
                    if og == 0 and d < 4:
                        wait_buf(q_eng, ("wqk_f", 0))
                    g.do(q_eng, lambda e, og=og, d=d, sp=sp: e.dma_start(
                        wvf[sp][:], wv_r[d * 128:(d + 1) * 128,
                                         og * 512:(og + 1) * 512]),
                        inc=f"dwv{sp}")
                    g.wait("vector", f"dwv{sp}")
                    wait_buf("vector", ("wvh", sp))
                    if og == 0 and d < 4:
                        wait_buf("vector", ("wqk_h", 0))
                    wc = g.do("vector", lambda e, sp=sp: e.tensor_copy(
                        wvh[sp][:], wvf[sp][:]), inc="cv")
                    set_free(("wvf", sp), ("cv", wc))
                    g.wait("tensor", "cv", wc)
                    vmm = 0
                    for tc in range(8):
                        if d == 0:
                            wait_bank("tensor", tc)
                        vmm = g.do("tensor", lambda e, d=d, tc=tc, sp=sp:
                                   e.matmul(
                            pband[tc][:],
                            x_h[:, d * 1024 + tc * 128:d * 1024 + tc * 128 + 128],
                            wvh[sp][:],
                            start=(d == 0), stop=(d == DC - 1)),
                            inc="ct" if tc == 7 else None)
                    d_mm[d] = vmm
                    set_free(("wvh", sp), ("ct", vmm))
                    if Hh == 0 and og == 3:
                        xp = d % 2
                        if d < 2:
                            wait_buf("sync", ("wqk_f", 1))
                        wait_buf("sync", ("xs2", xp))
                        g.do("sync", lambda e, d=d, xp=xp: e.dma_start(
                            xsf2[xp][:], xT[d * 128:(d + 1) * 128, 1024:2048]),
                            inc=f"dxt{xp}")
                        g.wait("vector", f"dxt{xp}")
                        g.wait("vector", "ct", vmm)
                        xc = g.do("vector", lambda e, d=d, xp=xp: e.tensor_copy(
                            x_h[:, d * 1024:(d + 1) * 1024], xsf2[xp][:]),
                            inc="cv")
                        set_free(("xs2", xp), ("cv", xc))
                        x_ready_h1 = xc
                g.wait("scalar", "ct", d_mm[DC - 1])
                for pr in range(4):
                    col = (og * 16 + Hh * 8 + pr * 2) * 512
                    ev = g.do("scalar", lambda e, pr=pr, col=col: e.activation(
                        v_flat[:, col:col + 1024], pT[pr][:], AF.Copy),
                        inc="cs")
                    set_banks([pr * 2, pr * 2 + 1], ("cs", ev))
            if Hh == 0:
                set_free(("wqk_f", 0), ("cv", g.c["cv"]))
                set_free(("wqk_f0h", 0), ("cv", g.c["cv"]))
                set_free(("wqk_f1h", 0), ("cv", g.c["cv"]))
                set_free(("wqk_h", 0), ("ct", d_mm[DC - 1]))
                set_free(("wqk_f", 1), ("cv", x_ready_h1))
                set_free(("wqk_f0h", 1), ("cv", x_ready_h1))
                set_free(("wqk_f1h", 1), ("cv", x_ready_h1))

        g.fence("vector", "cv")
        g.do("vector", lambda e: e.tensor_tensor(
            nmax32[:], normmax[:, 0:64:2], normmax[:, 1:64:2], ALU.max),
            inc="cv")
        g.fence("vector", "cv")
        pr_cv = g.do("vector", lambda e: e.tensor_tensor(
            prod[:], nmax32[:, 0:16], nmax32[:, 16:32], ALU.mult), inc="cv")
        g.wait("scalar", "cv", pr_cv)
        g.do("scalar", lambda e: e.activation(mbias[:], prod[:], AF.Ln),
             inc="cs")
        g.fence("scalar", "cs")
        g.do("scalar", lambda e: e.activation(mbias[:], mbias[:], AF.Exp,
                                              scale=0.5, bias=cln_t[:]),
             inc="cs")
        g.wait("vector", "cs")
        mbias_done = g.do("vector", lambda e: e.tensor_scalar_mul(
            mbias[:], mbias[:], -1.0), inc="cv")

        a_end_cs = g.c["cs"]

        if DEBUG:
            g.wait("gpsimd", "cv", mbias_done)
            g.wait("gpsimd", "cs", a_end_cs)
            g.do("gpsimd", lambda e: e.dma_start(dbg_k[:], k_all[:]), inc="dg")
            g.do("gpsimd", lambda e: e.dma_start(dbg_v[:], v_flat[:]),
                 inc="dg")
            g.do("gpsimd", lambda e: e.dma_start(dbg_mb[:], mbias[:]),
                 inc="dg")
            g.do("gpsimd", lambda e: e.dma_start(dbg_r[:], nmax32[:]),
                 inc="dg")
            g.do("gpsimd", lambda e: e.dma_start(dbg_p[:], prod[:]), inc="dg")

        sb.free("x_h", "wqk_f0", "wqk_f1", "wqk_h0", "wqk_h1",
                "g16_0", "g16_1", "qs16_0", "qs16_1", "sqbuf",
                "qo0", "qo1", "normmax", "nmax32", "prod")

        if PHASES < 2:
            g.wait("gpsimd", "dg")
            g.wait("sync", "sq0", g.c["sq0"])
            g.wait("sync", "sq1", g.c["sq1"])
            g.wait("sync", "ds")
            g.wait("tensor", "ct")
            g.wait("vector", "cv")
            g.wait("scalar", "cs")
            return _finish(nc, g, block)
        q_sb = [sb.alloc(f"q_sb{i}", [128, T], F16) for i in range(2)]
        pt_b = [sb.alloc(f"pt_b{i}", [128, 1024], BF16) for i in range(3)]
        rlbc = sb.alloc("rlbc", [128, 512], F32)
        ostg = [sb.alloc(f"ostg{i}", [128, 512], F16) for i in range(2)]
        SP = [0, 1, 3]

        g.wait("tensor", "cs", a_end_cs)
        g.wait("tensor", "cv", mbias_done)
        qsp_tot = (g.c["sq0"], g.c["sq1"])
        for eng in ("vector", "scalar"):
            g.wait(eng, "sq0", qsp_tot[0])
            g.wait(eng, "sq1", qsp_tot[1])

        for h in range(NH):
            g.wait("sync", "sq0", qsp_tot[0])
            g.wait("sync", "sq1", qsp_tot[1])
            wait_buf("sync", ("q_sb", h % 2))
            qld = g.do("sync", lambda e, h=h: e.dma_start(
                q_sb[h % 2][:], qsp[h, :, :]), inc=f"dq{h % 2}")
            for qg in range(4):
                it = h * 4 + qg
                og, hs = h // 4, h % 4
                g.wait("tensor", f"dq{h % 2}", qld)
                nkt = 4 * qg + 4
                npair = nkt // 2
                av_done = 0

                def meta(kt):
                    qlo = max(kt * 128, 512 * qg)
                    nv = 512 * (qg + 1) - qlo
                    return qlo, nv, qlo - 512 * qg, kt % 2

                pr_ready = {}

                def emit_spair(pp):
                    sp = SP[pp % 3]
                    pi = pp % 3
                    wait_banks("tensor", [2 * sp, 2 * sp + 1])
                    smm = 0
                    for i2 in (0, 1):
                        kt = 2 * pp + i2
                        qlo, nv, coff, half = meta(kt)
                        smm = g.do("tensor", lambda e, h=h, kt=kt, qlo=qlo,
                                   nv=nv, half=half, sp=sp: e.matmul(
                            pT[sp][:, half * 512:half * 512 + nv],
                            k_all[:, h * T + kt * 128:h * T + kt * 128 + 128],
                            q_sb[h % 2][:, qlo:qlo + nv],
                            start=True, stop=True), inc="ct")
                    g.wait("scalar", "ct", smm)
                    wait_buf("scalar", ("pt", pi))
                    expd = g.do("scalar", lambda e, h=h, sp=sp, pi=pi:
                                e.activation(
                        pt_b[pi][:], pT[sp][:], AF.Exp,
                        bias=mbias[:, h:h + 1], scale=ISQ), inc="cs")
                    set_banks([2 * sp, 2 * sp + 1], ("cs", expd))
                    ready = ("cs", expd)
                    for i2 in (0, 1):
                        kt = 2 * pp + i2
                        if kt >= 4 * qg:
                            g.wait("vector", "cs", expd)
                            mskd = g.do("vector", lambda e, pi=pi, half=kt % 2:
                                        e.tensor_tensor(
                                pt_b[pi][:, half * 512:half * 512 + 128],
                                pt_b[pi][:, half * 512:half * 512 + 128],
                                tri_bf[:], ALU.mult), inc="cv")
                            ready = ("cv", mskd)
                    pr_ready[pp] = ready

                for pp0 in range(min(3, npair)):
                    emit_spair(pp0)
                wait_bank("tensor", 4)
                wait_bank("tensor", 5)
                for pp in range(npair):
                    g.wait("tensor", *pr_ready[pp])
                    pi = pp % 3
                    for i2 in (0, 1):
                        kt = 2 * pp + i2
                        qlo, nv, coff, half = meta(kt)
                        vcol = (og * 16 + kt) * 512 + hs * 128
                        g.do("tensor", lambda e, vcol=vcol, coff=coff, nv=nv,
                             kt=kt, half=half, nkt=nkt, pi=pi: e.matmul(
                            pband[4][:, coff:coff + nv],
                            v_flat[:, vcol:vcol + 128],
                            pt_b[pi][:, half * 512:half * 512 + nv],
                            start=(kt == 0), stop=(kt == nkt - 1)), inc="ct")
                        av_done = g.do("tensor", lambda e, coff=coff, nv=nv,
                                       kt=kt, half=half, nkt=nkt, pi=pi:
                                       e.matmul(
                            pband[5][:, coff:coff + nv],
                            ones_bf[:],
                            pt_b[pi][:, half * 512:half * 512 + nv],
                            start=(kt == 0), stop=(kt == nkt - 1)), inc="ct")
                    set_free(("pt", pi), ("ct", av_done))
                    if pp + 3 < npair:
                        emit_spair(pp + 3)
                g.wait("scalar", "ct", av_done)
                wait_buf("scalar", ("rlbc",))
                g.do("scalar", lambda e: e.activation(
                    rlbc[:], pband[5][:], AF.Ln), inc="cs")
                bank_free[5] = ("cs", g.c["cs"])
                g.fence("scalar", "cs")
                rld = g.do("scalar", lambda e: e.activation(
                    rlbc[:], rlbc[:], AF.Exp, scale=-1.0), inc="cs")
                g.wait("vector", "cs", rld)
                so = it % 2
                wait_buf("vector", ("ostg", so))
                odone = g.do("vector", lambda e, so=so:
                             e.tensor_tensor(ostg[so][:], pband[4][:], rlbc[:],
                                             ALU.mult), inc="cv")
                set_free(("rlbc",), ("cv", odone))
                bank_free[4] = ("cv", odone)
                g.wait("gpsimd", "cv", odone)
                osd = 0
                for i in range(4):
                    tci = qg * 4 + i
                    osd = g.do("gpsimd", lambda e, tci=tci, h=h, so=so, i=i:
                               e.dma_start(
                        osp[tci, :, h * 128:(h + 1) * 128],
                        ostg[so][:, i * 128:(i + 1) * 128]), inc=f"so{so}")
                set_free(("ostg", so), (f"so{so}", osd))
            set_free(("q_sb", h % 2), ("ct", g.c["ct"]))

        osp_tot = (g.c["so0"], g.c["so1"])

        sb.free("q_sb0", "q_sb1", "pt_b0", "pt_b1", "pt_b2", "rlbc",
                "ostg0", "ostg1", "k_all", "v_flat")

        if PHASES < 3:
            g.wait("gpsimd", "dg")
            g.wait("sync", "sq0", g.c["sq0"])
            g.wait("sync", "sq1", g.c["sq1"])
            g.wait("sync", "so0", g.c["so0"])
            g.wait("sync", "so1", g.c["so1"])
            g.wait("sync", "ds")
            g.wait("tensor", "ct")
            g.wait("vector", "cv")
            g.wait("scalar", "cs")
            return _finish(nc, g, block)
        b_end_ct = g.c["ct"]
        b_end_cv = g.c["cv"]
        g.wait("sync", "ct", b_end_ct)
        g.wait("sync", "cv", b_end_cv)
        for eng in ("sync", "scalar"):
            g.wait(eng, "so0", g.c["so0"])
            g.wait(eng, "so1", g.c["so1"])
            g.wait(eng, "sq0", g.c["sq0"])
            g.wait(eng, "sq1", g.c["sq1"])
        wo_h = sb.alloc("wo_h", [128, DC * 2048], F16)
        wffn = sb.alloc("wffn", [128, DC * 2048], F16)
        wof = [sb.alloc(f"wof{i}", [128, 2048], F32) for i in range(2)]
        y2 = sb.alloc("y2", [128, D], F32)
        oc = [sb.alloc(f"oc{i}", [128, 2048], F16) for i in range(2)]
        qfm = sb.alloc("qfm", [128, D], F16)
        m32 = sb.alloc("m32", [128, D], F32)
        i32a = sb.alloc("i32a", [128, D], I32)
        q32 = m32
        f32t = sb.alloc("f16t", [128, D], F16)
        idn32 = sb.alloc("idn32", [128, 128], F32)
        wsc16 = sb.alloc("wsc16", [128, D], F16)
        bia16 = sb.alloc("bia16", [128, D], F16)
        bn24 = sb.alloc("bn24", [128, 24], F32)
        m8 = sb.alloc("m8", [128, 8], F32)
        st_m = sb.alloc("st_m", [128, 1], F32)
        st_v = sb.alloc("st_v", [128, 1], F32)
        st_i = sb.alloc("st_i", [128, 1], F32)
        st_t = sb.alloc("st_t", [128, 1], F32)
        st_u = sb.alloc("st_u", [128, 1], F32)
        am = sb.alloc("am", [128, 1], F32)
        scq2 = [sb.alloc(f"scq{i}", [128, 1], F32) for i in range(2)]
        invq = sb.alloc("invq", [128, 1], F32)
        negmi = sb.alloc("negmi", [128, 1], F32)

        g.do("sync", lambda e: e.dma_start(m32[:], wsc_h[:]), inc="dwb")
        g.do("sync", lambda e: e.dma_start(y2[:], bia_h[:]), inc="dwb")
        g.do("sync", lambda e: e.dma_start(idn32[:], idn_h[:]), inc="dwb")
        g.wait("tensor", "dwb")
        g.wait("scalar", "dwb")
        g.do("scalar", lambda e: e.activation(wsc16[:], m32[:], AF.Copy),
             inc="cs")
        cbc_cs = g.do("scalar", lambda e: e.activation(bia16[:], y2[:],
                                                       AF.Copy), inc="cs")
        set_free(("y12",), ("cs", cbc_cs))
        for kd in range(DC):
            g.do("sync", lambda e, kd=kd: e.dma_start(
                wffn[:, kd * 2048:(kd + 1) * 2048],
                wffn_r[kd * 128:(kd + 1) * 128, :]), inc="dff")
        wo_cs = 0
        wo_cs_of = {}
        wo_cv_of = {}
        for idx in range(DC):
            p = idx % 2
            wait_buf("sync", ("wof", p))
            g.do("sync", lambda e, idx=idx, p=p: e.dma_start(
                wof[p][:], wo_r[idx * 128:(idx + 1) * 128, :]), inc=f"dwo{p}")
            if p == 0:
                g.wait("scalar", f"dwo{p}")
                wo_cs = g.do("scalar", lambda e, idx=idx, p=p: e.activation(
                    wo_h[:, idx * 2048:(idx + 1) * 2048], wof[p][:], AF.Copy),
                    inc="cs")
                wo_cs_of[idx] = ("cs", wo_cs)
                set_free(("wof", p), ("cs", wo_cs))
            else:
                g.wait("vector", f"dwo{p}")
                wcv = g.do("vector", lambda e, idx=idx, p=p: e.tensor_copy(
                    wo_h[:, idx * 2048:(idx + 1) * 2048], wof[p][:]),
                    inc="cv")
                wo_cs_of[idx] = ("cv", wcv)
                set_free(("wof", p), ("cv", wcv))
        wld_ff = g.c["dff"]

        sb.free("wof0", "wof1")
        xtm = [sb.alloc(f"xtm{i}", [128, D], F32) for i in range(2)]
        set_free(("xtm", 0), wo_cs_of[DC - 2], wo_cs_of[DC - 1])
        set_free(("xtm", 1), wo_cs_of[DC - 2], wo_cs_of[DC - 1])

        def ln_stats(src_h):
            g.fence("vector", "cv")
            for j in range(4):
                g.do("vector", lambda e, j=j: e.bn_stats(
                    bn24[:, j * 6:(j + 1) * 6],
                    src_h[:, j * 512:(j + 1) * 512]), inc="cv")
            g.fence("vector", "cv")
            g.do("vector", lambda e: e.tensor_tensor(
                m8[:], bn24[:, 1:24:3], bn24[:, 1:24:3], ALU.mult), inc="cv")
            g.do("vector", lambda e: e.tensor_reduce(
                st_m[:], bn24[:, 1:24:3], AX.X, ALU.add), inc="cv")
            g.fence("vector", "cv")
            g.do("vector", lambda e: e.tensor_scalar_mul(st_m[:], st_m[:],
                                                         1.0 / 8.0), inc="cv")
            g.do("vector", lambda e: e.tensor_reduce(
                st_t[:], m8[:], AX.X, ALU.add), inc="cv")
            g.do("vector", lambda e: e.tensor_reduce(
                st_v[:], bn24[:, 2:24:3], AX.X, ALU.add), inc="cv")
            g.fence("vector", "cv")
            g.do("vector", lambda e: e.scalar_tensor_tensor(
                st_v[:], st_v[:], 1.0 / D, st_t[:], ALU.mult, ALU.bypass),
                 inc="cv")
            g.fence("vector", "cv")
            g.do("vector", lambda e: e.scalar_tensor_tensor(
                st_v[:], st_t[:], 1.0 / 8.0, st_v[:], ALU.mult, ALU.add),
                 inc="cv")
            g.do("vector", lambda e: e.tensor_tensor(
                st_u[:], st_m[:], st_m[:], ALU.mult), inc="cv")
            g.fence("vector", "cv")
            vstat = g.do("vector", lambda e: e.tensor_tensor(
                st_v[:], st_v[:], st_u[:], ALU.subtract), inc="cv")
            g.wait("scalar", "cv", vstat)
            g.fence("scalar", "cs")
            g.do("scalar", lambda e: e.activation(st_i[:], st_v[:], AF.Ln,
                                                  bias=eps_t[:]), inc="cs")
            g.fence("scalar", "cs")
            g.do("scalar", lambda e: e.activation(st_i[:], st_i[:], AF.Exp,
                                                  scale=-0.5), inc="cs")
            g.fence("scalar", "cs")
            g.wait("vector", "cs")
            g.fence("vector", "cv")

        qd_of = {}
        womm_of = {}
        std_of = {}

        def emit_loads(tc):
            pc = tc % 2
            g.wait("sync", "so0", osp_tot[0])
            g.wait("sync", "so1", osp_tot[1])
            wait_buf("sync", ("y12",))
            wait_buf("sync", ("oc", pc))
            g.do("sync", lambda e, tc=tc, pc=pc: e.dma_start(
                oc[pc][:], osp[tc, :, :]), inc=f"doc{pc}")
            g.wait("gpsimd", "so0", osp_tot[0])
            g.wait("gpsimd", "so1", osp_tot[1])
            wait_buf("gpsimd", ("y12",))
            wait_buf("gpsimd", ("xtm", pc))
            g.do("gpsimd", lambda e, tc=tc, pc=pc: e.dma_start(
                xtm[pc][:], x_tm[tc * 128:(tc + 1) * 128, :]), inc=f"dxg{pc}")
            return (g.c[f"dxg{pc}"], g.c[f"doc{pc}"])

        def emit_womm(tc, cld):
            pc = tc % 2
            g.wait("tensor", f"doc{pc}", cld[1])
            for gg in range(4):
                wait_bank("tensor", gg)
            womm = 0
            for idx in range(DC):
                g.wait("tensor", *wo_cs_of[idx])
                for gg in range(4):
                    womm = g.do("tensor", lambda e, idx=idx, gg=gg, pc=pc:
                                e.matmul(
                        pband[gg][:],
                        oc[pc][:, idx * 128:(idx + 1) * 128],
                        wo_h[:, idx * 2048 + gg * 512:idx * 2048 + gg * 512 + 512],
                        start=(idx == 0), stop=(idx == DC - 1)),
                        inc="ct" if idx == DC - 1 else None)
            set_free(("oc", pc), ("ct", womm))
            womm_of[tc] = womm

        def emit_evac_wo(tc, cld):
            pc = tc % 2
            g.wait("vector", "ct", womm_of[tc])
            g.wait("vector", f"dxg{pc}", cld[0])
            for gg in range(4):
                ev = g.do("vector", lambda e, gg=gg, pc=pc: e.tensor_tensor(
                    xtm[pc][:, gg * 512:(gg + 1) * 512], pband[gg][:],
                    xtm[pc][:, gg * 512:(gg + 1) * 512], ALU.add), inc="cv")
                bank_free[gg] = ("cv", ev)

        def emit_evac_ln_quant(tc):
            pc = tc % 2
            yab = xtm[pc]
            ln_stats(yab)
            g.do("vector", lambda e, yab=yab: e.tensor_scalar(
                yab[:], yab[:], st_m[:], st_i[:], ALU.subtract, ALU.mult),
                inc="cv")
            g.fence("vector", "cv")
            g.do("vector", lambda e, yab=yab: e.tensor_reduce(
                am[:], yab[:], AX.X, ALU.max, apply_absolute_value=True),
                inc="cv")
            g.fence("vector", "cv")
            g.do("vector", lambda e: e.tensor_scalar_max(am[:], am[:], EPS_Q),
                 inc="cv")
            g.fence("vector", "cv")
            g.do("vector", lambda e, tc=tc: e.tensor_scalar_mul(
                scq2[tc % 2][:], am[:], 1.0 / 127.0), inc="cv")
            g.do("vector", lambda e: e.reciprocal(invq[:], am[:]), inc="cv")
            g.fence("vector", "cv")
            g.do("vector", lambda e: e.tensor_scalar_mul(
                invq[:], invq[:], 127.0), inc="cv")
            g.fence("vector", "cv")
            if tc - 1 in trd_of:
                g.wait("vector", "ct", trd_of[tc - 1])
            tq = g.do("vector", lambda e, yab=yab: e.tensor_scalar(
                m32[:], yab[:], invq[:], 256.0, ALU.mult, ALU.add), inc="cv")
            g.fence("vector", "cv")
            g.do("vector", lambda e: e.tensor_copy(i32a[:], m32[:]), inc="cv")
            g.fence("vector", "cv")
            g.do("vector", lambda e: e.tensor_tensor(
                m32[:], m32[:], i32a[:], ALU.subtract), inc="cv")
            g.fence("vector", "cv")
            qd_of[tc] = g.do("vector", lambda e: e.scalar_tensor_tensor(
                q32[:], m32[:], 0.5, i32a[:], ALU.is_ge, ALU.add), inc="cv")

        trd_of = {}

        fmm_of = {}

        def emit_post_tensor(tc):
            g.wait("tensor", "cv", qd_of[tc])
            wait_banks("tensor", [4, 5, 6, 7])
            trd0 = 0
            for kd in range(16):
                if kd == 8:
                    trd0 = g.c["ct"]
                trd = g.do("tensor", lambda e, kd=kd: e.transpose(
                    pT[2 + kd // 8][:, (kd % 8) * 128:(kd % 8) * 128 + 128],
                    q32[:, kd * 128:(kd + 1) * 128], idn32[:]), inc="ct")
            trd_of[tc] = trd
            g.wait("scalar", "ct", trd0)
            qs0 = g.do("scalar", lambda e: e.activation(
                qfm[:, 0:1024], pT[2][:], AF.Copy, bias=-256.0), inc="cs")
            set_banks([4, 5], ("cs", qs0))
            g.wait("scalar", "ct", trd)
            qs1 = g.do("scalar", lambda e: e.activation(
                qfm[:, 1024:2048], pT[3][:], AF.Copy, bias=-256.0), inc="cs")
            set_banks([6, 7], ("cs", qs1))
            g.wait("tensor", "dff", wld_ff)
            for rr in range(2):
                bb = 4 + rr * 2
                for gg in range(2):
                    wait_bank("tensor", bb + gg)
                fmm = 0
                for kd in range(DC):
                    if kd == 0 and rr == 0:
                        g.wait("tensor", "cs", qs0)
                    if kd == 8 and rr == 0:
                        g.wait("tensor", "cs", qs1)
                    for gg in range(2):
                        ng = rr * 2 + gg
                        fmm = g.do("tensor", lambda e, kd=kd, gg=gg, ng=ng,
                                   bb=bb: e.matmul(
                            pband[bb + gg][:],
                            qfm[:, kd * 128:(kd + 1) * 128],
                            wffn[:, kd * 2048 + ng * 512:kd * 2048 + ng * 512 + 512],
                            start=(kd == 0), stop=(kd == DC - 1)),
                            inc="ct" if kd == DC - 1 else None)
                fmm_of[(tc, rr)] = fmm

        def emit_post_vector(tc):
            if tc - 1 in std_of:
                g.wait("vector", "dg", std_of[tc - 1])
            sq2 = scq2[tc % 2]
            for rr in range(2):
                bb = 4 + rr * 2
                g.wait("vector", "ct", fmm_of[(tc, rr)])
                for gg in range(2):
                    ng = rr * 2 + gg
                    g.do("vector", lambda e, gg=gg, ng=ng, bb=bb:
                         e.tensor_tensor(
                        y2[:, ng * 512:(ng + 1) * 512], pband[bb + gg][:],
                        wsc16[:, ng * 512:(ng + 1) * 512], ALU.mult), inc="cv")
                g.fence("vector", "cv")
                for gg in range(2):
                    ng = rr * 2 + gg
                    fev = g.do("vector", lambda e, gg=gg, ng=ng, sq2=sq2:
                               e.scalar_tensor_tensor(
                        f32t[:, ng * 512:(ng + 1) * 512],
                        y2[:, ng * 512:(ng + 1) * 512], sq2[:],
                        bia16[:, ng * 512:(ng + 1) * 512],
                        ALU.mult, ALU.add), inc="cv")
                    bank_free[bb + gg] = ("cv", fev)
            g.fence("vector", "cv")
            yprev = xtm[tc % 2]
            ya_done = g.do("vector", lambda e, yprev=yprev: e.tensor_tensor(
                y2[:], yprev[:], f32t[:], ALU.add), inc="cv")
            set_free(("xtm", tc % 2), ("cv", ya_done))
            ln_stats(y2)
            y2d = g.do("vector", lambda e: e.tensor_scalar(
                y2[:], y2[:], st_m[:], st_i[:], ALU.subtract, ALU.mult),
                inc="cv")
            g.wait("gpsimd", "cv", y2d)
            std_of[tc] = g.do("gpsimd", lambda e, tc=tc: e.dma_start(
                out[tc * 128:(tc + 1) * 128, :], y2[:]), inc="dg")

        for it in range(18):
            if it >= 2:
                emit_post_tensor(it - 2)
            if 0 <= it - 1 < 16:
                emit_evac_ln_quant(it - 1)
            if it >= 2:
                emit_post_vector(it - 2)
            if it < 16:
                cld = emit_loads(it)
                emit_womm(it, cld)
                emit_evac_wo(it, cld)

        g.wait("gpsimd", "dg")
        g.wait("sync", "ds")
        g.wait("sync", "dg")
        g.wait("sync", "sq0", g.c["sq0"])
        g.wait("sync", "sq1", g.c["sq1"])
        g.wait("sync", "so0", g.c["so0"])
        g.wait("sync", "so1", g.c["so1"])
        return _finish(nc, g, block)


def _finish(nc, g, block):
    @block.sync
    def _(e):
        g.run("sync", e)

    @block.gpsimd
    def _(e):
        g.run("gpsimd", e)

    @block.scalar
    def _(e):
        g.run("scalar", e)

    @block.vector
    def _(e):
        g.run("vector", e)

    @block.tensor
    def _(e):
        g.run("tensor", e)

    return nc


def _host_prep(L0_attn_x, L0_attn_Wqkv, L0_attn_WO, L0_ffn_weight,
               L0_ffn_w_scale, L0_ffn_bias):
    x = np.ascontiguousarray(np.asarray(L0_attn_x, dtype=np.float32))
    Wqkv = np.ascontiguousarray(np.asarray(L0_attn_Wqkv, dtype=np.float32))
    WO = np.ascontiguousarray(np.asarray(L0_attn_WO, dtype=np.float32))
    wffn = np.asarray(L0_ffn_weight)
    wsc = np.asarray(L0_ffn_w_scale, dtype=np.float32)
    bia = np.asarray(L0_ffn_bias, dtype=np.float32)

    Wqk = Wqkv[:2 * D].reshape(32, 128, DC, 128).transpose(0, 3, 2, 1)
    wqk_t = np.ascontiguousarray(Wqk).reshape(32, 128, 2048)
    wv_r = np.ascontiguousarray(Wqkv[2 * D:].T)
    wo_r = np.ascontiguousarray(WO.T)
    wffn_r = np.ascontiguousarray(wffn.T).astype(np.float16)

    pos = np.arange(T, dtype=np.float32)
    freqs = np.exp((-math.log(10000.0) / DH) *
                   np.arange(0, DH, 2, dtype=np.float32)).astype(np.float32)
    ang = (freqs[:, None] * pos[None, :]).astype(np.float32)
    c64 = np.cos(ang.astype(np.float64)).astype(np.float32)
    s64 = np.sin(ang.astype(np.float64)).astype(np.float32)
    ropec = np.repeat(c64, 2, axis=0)
    ropes_ = np.empty((128, T), np.float32)
    ropes_[0::2] = -s64
    ropes_[1::2] = s64

    ii, jj = np.meshgrid(np.arange(128), np.arange(128), indexing="ij")
    tri = (ii <= jj).astype(np.float32)
    rsw = ((ii ^ 1) == jj).astype(np.float32)
    idn = np.eye(128, dtype=np.float32)

    wsc_bc = np.ascontiguousarray(
        np.broadcast_to(wsc[:, 0][None, :], (128, D))).astype(np.float32)
    bia_bc = np.ascontiguousarray(
        np.broadcast_to(bia[None, :], (128, D))).astype(np.float32)

    shared = dict(wqk_t=wqk_t, wv_r=wv_r, wo_r=wo_r, wffn_r=wffn_r,
                  ropec=ropec, ropes=ropes_, tri_h=tri, rsw_h=rsw, idn_h=idn,
                  wsc_h=wsc_bc, bia_h=bia_bc)
    in_maps = []
    for b in range(NCORES):
        m = dict(shared)
        m["xT"] = np.ascontiguousarray(x[b].T)
        m["x_tm"] = x[b]
        in_maps.append(m)
    return in_maps


def kernel(L0_attn_x, L0_attn_Wqkv, L0_attn_WO, L0_ffn_weight,
           L0_ffn_w_scale, L0_ffn_bias):
    in_maps = _host_prep(L0_attn_x, L0_attn_Wqkv, L0_attn_WO, L0_ffn_weight,
                         L0_ffn_w_scale, L0_ffn_bias)
    nc = build_program()
    res = run_bass_kernel_spmd(nc, in_maps, core_ids=list(range(NCORES)))
    return np.stack([np.asarray(res.results[b]["out"], dtype=np.float32)
                     for b in range(NCORES)], axis=0)


# version 24
# speedup vs baseline: 1.0440x; 1.0440x over previous
import math
import numpy as np
import concourse.bass as bass
import concourse.mybir as mybir
from concourse.bass_utils import run_bass_kernel_spmd

F32 = mybir.dt.float32
F16 = mybir.dt.float16
BF16 = mybir.dt.bfloat16
I32 = mybir.dt.int32
U8 = mybir.dt.uint8
AF = mybir.ActivationFunctionType
ALU = mybir.AluOpType
AX = mybir.AxisListType

D = 2048
T = 2048
NH = 16
DH = 128
DC = 16
NCORES = 8
ISQ = 1.0 / math.sqrt(DH)
EPS_LN = 1e-5
EPS_Q = 1e-7
CC_BOUND = (1.03 * (1.0 / math.sqrt(128))) ** 2
CLN_HALF = 0.5 * math.log(CC_BOUND)

PHASES = 3
DEBUG = False

ENGINES = ("sync", "gpsimd", "scalar", "vector", "tensor")
DMA_SEMS = ("ds", "dg", "sq0", "sq1", "so0", "so1", "dcn",
            "dwq0", "dwq1", "dwg0", "dwg1", "dws0", "dws1",
            "dwv0", "dwv1", "dwv2", "dwv3",
            "dq0", "dq1", "dxt0", "dxt1", "dxt2", "dxt3",
            "doc0", "doc1", "dwo0", "dwo1", "dff", "dwb", "dxg0", "dxg1")
ALL_SEMS = DMA_SEMS + ("cs", "cv", "ct")


class Gen:

    def __init__(self):
        self.ops = {e: [] for e in ENGINES}
        self.c = {s: 0 for s in ALL_SEMS}
        self.sems = {}
        self.last_wait = {(e, s): 0 for e in ENGINES for s in ALL_SEMS}

    def do(self, eng, fn, inc=None):
        if inc is None:
            self.ops[eng].append((fn, None, 0))
        else:
            n = 16 if inc in DMA_SEMS else 1
            self.c[inc] += n
            self.ops[eng].append((fn, inc, n))
        return self.c.get(inc, 0)

    def wait(self, eng, sem, val=None):
        v = self.c[sem] if val is None else val
        if v <= 0 or self.last_wait[(eng, sem)] >= v:
            return
        self.last_wait[(eng, sem)] = v
        self.ops[eng].append((lambda e, s=sem, vv=v: e.wait_ge(self.sems[s], vv),
                              None, 0))

    def fence(self, eng, sem):
        self.wait(eng, sem, self.c[sem])

    def run(self, eng_name, e):
        for fn, inc, n in self.ops[eng_name]:
            ins = fn(e)
            if inc is not None:
                ins.then_inc(self.sems[inc], n)


class SBAlloc:

    def __init__(self, nc, backing, base_addr, total):
        self.nc = nc
        self.backing = backing
        self.base = base_addr
        self.total = total
        self.free_list = [(0, total)]
        self.live = {}

    def alloc(self, name, shape, dtype):
        per_part = int(np.prod(shape[1:])) * mybir.dt.size(dtype)
        per_part = (per_part + 31) // 32 * 32
        for i, (s, sz) in enumerate(self.free_list):
            if sz >= per_part:
                if sz == per_part:
                    del self.free_list[i]
                else:
                    self.free_list[i] = (s + per_part, sz - per_part)
                h = self.nc.virtual_sbuf_tensor(name, list(shape), dtype)
                self.nc._tensor(h.name, h.shape, h.dtype, type="SB")
                mloc = self.nc.lookup_mloc(h)
                mloc.allocated = True
                mloc.addr = self.base + s
                self.live[name] = (s, per_part)
                return h
        raise MemoryError(f"SBAlloc: no room for {name} ({per_part} B); "
                          f"free={self.free_list}")

    def free(self, *names):
        for name in names:
            s, sz = self.live.pop(name)
            self.free_list.append((s, sz))
        self.free_list.sort()
        out = []
        for seg in self.free_list:
            if out and out[-1][0] + out[-1][1] == seg[0]:
                out[-1] = (out[-1][0], out[-1][1] + seg[1])
            else:
                out.append(seg)
        self.free_list = out


def build_program():
    nc = bass.Bass("TRN2", target_bir_lowering=False, debug=False)

    xT = nc.dram_tensor("xT", [D, T], F32, kind="ExternalInput")
    x_tm = nc.dram_tensor("x_tm", [T, D], F32, kind="ExternalInput")
    wqk_t = nc.dram_tensor("wqk_t", [32, 128, 2048], F32, kind="ExternalInput")
    wv_r = nc.dram_tensor("wv_r", [D, D], F32, kind="ExternalInput")
    wo_r = nc.dram_tensor("wo_r", [D, D], F32, kind="ExternalInput")
    wffn_r = nc.dram_tensor("wffn_r", [D, D], F16, kind="ExternalInput")
    ropec = nc.dram_tensor("ropec", [128, T], F32, kind="ExternalInput")
    ropes = nc.dram_tensor("ropes", [128, T], F32, kind="ExternalInput")
    tri_h = nc.dram_tensor("tri_h", [128, 128], F32, kind="ExternalInput")
    rsw_h = nc.dram_tensor("rsw_h", [128, 128], F32, kind="ExternalInput")
    idn_h = nc.dram_tensor("idn_h", [128, 128], F32, kind="ExternalInput")
    wsc_h = nc.dram_tensor("wsc_h", [128, D], F32, kind="ExternalInput")
    bia_h = nc.dram_tensor("bia_h", [128, D], F32, kind="ExternalInput")
    out = nc.dram_tensor("out", [T, D], F32, kind="ExternalOutput")

    scratch_kind = "ExternalOutput" if DEBUG else "Internal"
    qsp = nc.dram_tensor("qsp", [NH, 128, T], F16, kind=scratch_kind)
    osp = nc.dram_tensor("osp", [16, 128, NH * 128], F16, kind=scratch_kind)
    if DEBUG:
        dbg_k = nc.dram_tensor("dbg_k", [128, NH * T], F16,
                               kind="ExternalOutput")
        dbg_v = nc.dram_tensor("dbg_v", [128, 64 * 512], BF16,
                               kind="ExternalOutput")
        dbg_mb = nc.dram_tensor("dbg_mb", [128, 16], F32,
                                kind="ExternalOutput")
        dbg_r = nc.dram_tensor("dbg_r", [128, 32], F32, kind="ExternalOutput")
        dbg_p = nc.dram_tensor("dbg_p", [128, 16], F32, kind="ExternalOutput")

    g = Gen()
    from contextlib import ExitStack
    with ExitStack() as stack:
        for s in ALL_SEMS:
            g.sems[s] = stack.enter_context(nc.semaphore(s))
        block = stack.enter_context(nc.Block())

        ARENA = 212832
        backing = stack.enter_context(nc.sbuf_tensor("arena", [128, ARENA], U8))
        base_addr = nc.lookup_mloc(backing).addr
        sb = SBAlloc(nc, backing, base_addr, ARENA)

        pT = [stack.enter_context(nc.psum_tensor(f"pp{i}", [128, 1024], F32))
              for i in range(4)]
        pband = [pT[i // 2][:, (i % 2) * 512:(i % 2) * 512 + 512]
                 for i in range(8)]

        bank_free = {}

        def wait_bank(eng, b):
            if b in bank_free:
                sem, val = bank_free[b]
                g.wait(eng, sem, val)

        def wait_banks(eng, bs):
            for b in bs:
                wait_bank(eng, b)

        def set_banks(bs, cond):
            for b in bs:
                bank_free[b] = cond

        freed = {}

        def wait_buf(eng, key):
            for sem, val in freed.get(key, ()):
                g.wait(eng, sem, val)

        def set_free(key, *conds):
            freed[key] = conds

        cos_h = sb.alloc("cos_h", [128, T], F16)
        sin_h = sb.alloc("sin_h", [128, T], F16)
        tri_bf = sb.alloc("tri_bf", [128, 128], BF16)
        r16 = sb.alloc("r16", [128, 128], F16)
        ones_bf = sb.alloc("ones_bf", [128, 128], BF16)
        eps_t = sb.alloc("eps_t", [128, 1], F32)
        cln_t = sb.alloc("cln_t", [128, 1], F32)
        mbias = sb.alloc("mbias", [128, 16], F32)

        x_h = sb.alloc("x_h", [128, DC * 1024], F16)
        k_all = sb.alloc("k_all", [128, NH * T], F16)
        v_flat = sb.alloc("v_flat", [128, 64 * 512], BF16)
        wqk_f = [sb.alloc(f"wqk_f{i}", [128, 2048], F32) for i in range(2)]
        wqk_h = [sb.alloc(f"wqk_h{i}", [128, 2048], F16) for i in range(2)]
        wvf = [wqk_f[0][:, j * 512:(j + 1) * 512] for j in range(4)]
        wvh = [wqk_h[0][:, j * 512:(j + 1) * 512] for j in range(4)]
        xsf = [wqk_f[0][:, 0:1024], wqk_f[0][:, 1024:2048],
               wqk_f[1][:, 0:1024], wqk_f[1][:, 1024:2048]]
        xsf2 = [wqk_f[1][:, 0:1024], wqk_f[1][:, 1024:2048]]
        g16 = [sb.alloc(f"g16_{i}", [128, 1024], F16) for i in range(2)]
        qs16 = [sb.alloc(f"qs16_{i}", [128, 1024], F16) for i in range(2)]
        sqbuf = sb.alloc("sqbuf", [128, 1024], BF16)
        qo = [sb.alloc(f"qo{i}", [128, 1024], F16) for i in range(2)]
        normmax = sb.alloc("normmax", [128, 64], F32)
        nmax32 = sb.alloc("nmax32", [128, 32], F32)
        prod = sb.alloc("prod", [128, 16], F32)

        g.do("sync", lambda e: e.dma_start(wqk_f[0][:], ropec[:]), inc="dcn")
        g.do("sync", lambda e: e.dma_start(wqk_f[1][:], ropes[:]), inc="dcn")
        g.wait("vector", "dcn")
        g.do("vector", lambda e: e.tensor_copy(cos_h[:], wqk_f[0][:]), inc="cv")
        sin_cv = g.do("vector", lambda e: e.tensor_copy(sin_h[:], wqk_f[1][:]),
                      inc="cv")
        g.wait("sync", "cv", sin_cv)
        g.do("sync", lambda e: e.dma_start(wqk_f[0][:, 0:128], tri_h[:]),
             inc="dcn")
        g.do("sync", lambda e: e.dma_start(wqk_f[0][:, 128:256], rsw_h[:]),
             inc="dcn")
        g.wait("vector", "dcn")
        g.do("vector", lambda e: e.tensor_copy(tri_bf[:], wqk_f[0][:, 0:128]),
             inc="cv")
        g.do("vector", lambda e: e.tensor_copy(r16[:], wqk_f[0][:, 128:256]),
             inc="cv")
        g.do("vector", lambda e: e.memset(ones_bf[:], 1.0), inc="cv")
        g.do("vector", lambda e: e.memset(eps_t[:], EPS_LN), inc="cv")
        g.do("vector", lambda e: e.memset(cln_t[:], CLN_HALF), inc="cv")
        g.do("vector", lambda e: e.memset(normmax[:], 0.0), inc="cv")
        cv_consts = g.c["cv"]
        g.wait("tensor", "cv", cv_consts)
        g.wait("scalar", "cv", cv_consts)
        g.fence("vector", "cv")

        for d in range(DC):
            sp = d % 4
            if d == 0:
                g.wait("sync", "cv", cv_consts)
            wait_buf("sync", ("xs", sp))
            g.do("sync", lambda e, d=d, sp=sp: e.dma_start(
                xsf[sp][:], xT[d * 128:(d + 1) * 128, 0:1024]), inc=f"dxt{sp}")
            g.wait("vector", f"dxt{sp}")
            xc = g.do("vector", lambda e, d=d, sp=sp: e.tensor_copy(
                x_h[:, d * 1024:(d + 1) * 1024], xsf[sp][:]), inc="cv")
            set_free(("xs", sp), ("cv", xc))
        x_ready = g.c["cv"]
        for pp_ in range(2):
            set_free(("wqk_f", pp_), ("cv", x_ready))
            set_free(("wqk_f0h", pp_), ("cv", x_ready))
            set_free(("wqk_f1h", pp_), ("cv", x_ready))

        qk_par = 0
        ch_i = 0

        def emit_chain_pre(st):
            mainT = pT[st["par"]]
            mi = st["par"] * 2
            gi = st["gi"]
            g.wait("scalar", "ct", st["mm"])
            wait_buf("scalar", ("g16", gi))
            st["gready"] = g.do("scalar", lambda e, mainT=mainT, gi=gi:
                                e.activation(
                g16[gi][:], mainT[:], AF.Copy), inc="cs")
            wait_buf("scalar", ("sqb",))
            st["sqd"] = g.do("scalar", lambda e, mainT=mainT: e.activation(
                sqbuf[:], mainT[:], AF.Square), inc="cs")
            set_banks([mi, mi + 1], ("cs", st["sqd"]))

        def emit_chain_mid(st):
            o, t0h, gi = st["o"], st["t0h"], st["gi"]
            wait_banks("tensor", [4, 5])
            g.wait("tensor", "cs", st["gready"])
            g.do("tensor", lambda e, gi=gi: e.matmul(
                pband[4][:], r16[:], g16[gi][:, 0:512], start=True, stop=True),
                inc="ct")
            st["rmm"] = g.do("tensor", lambda e, gi=gi: e.matmul(
                pband[5][:], r16[:], g16[gi][:, 512:1024], start=True,
                stop=True), inc="ct")
            wait_banks("tensor", [6, 7])
            g.wait("tensor", "cs", st["sqd"])
            g.do("tensor", lambda e: e.matmul(
                pband[6][:], ones_bf[:], sqbuf[:, 0:512], start=True,
                stop=True), inc="ct")
            st["nmm"] = g.do("tensor", lambda e: e.matmul(
                pband[7][:], ones_bf[:], sqbuf[:, 512:1024], start=True,
                stop=True), inc="ct")
            set_free(("sqb",), ("ct", st["nmm"]))
            if o < 16:
                dst = qo[gi][:]
                wait_buf("vector", ("qo", gi))
            else:
                dst = k_all[:, (o - 16) * T + t0h:(o - 16) * T + t0h + 1024]
            st["dst"] = dst
            g.wait("vector", "cs", st["gready"])
            va = g.do("vector", lambda e, dst=dst, gi=gi, t0h=t0h:
                      e.tensor_tensor(
                dst, g16[gi][:], cos_h[:, t0h:t0h + 1024], ALU.mult),
                inc="cv")
            set_free(("g16", gi), ("cv", va), ("ct", st["rmm"]))

        def emit_chain_tail(st):
            o, t0h, gi, dst = st["o"], st["t0h"], st["gi"], st["dst"]
            g.wait("scalar", "ct", st["rmm"])
            wait_buf("scalar", ("qs16", gi))
            qsr = g.do("scalar", lambda e, gi=gi: e.activation(
                qs16[gi][:], pT[2][:], AF.Copy), inc="cs")
            set_banks([4, 5], ("cs", qsr))
            g.wait("vector", "cs", qsr)
            g.do("vector", lambda e, gi=gi, t0h=t0h: e.tensor_tensor(
                qs16[gi][:], qs16[gi][:], sin_h[:, t0h:t0h + 1024], ALU.mult),
                inc="cv")
            g.fence("vector", "cv")
            qdone = g.do("vector", lambda e, dst=dst, gi=gi: e.tensor_tensor(
                dst, dst, qs16[gi][:], ALU.add), inc="cv")
            set_free(("qs16", gi), ("cv", qdone))
            if o < 16:
                g.wait("scalar", "cv", qdone)
                qsd = g.do("scalar", lambda e, o=o, t0h=t0h, gi=gi: e.dma_start(
                    qsp[o, :, t0h:t0h + 1024], qo[gi][:]), inc=f"sq{gi}")
                set_free(("qo", gi), (f"sq{gi}", qsd))
            col = o * 2 + (t0h // 1024)
            g.wait("vector", "ct", st["nmm"])
            g.do("vector", lambda e, col=col: e.tensor_reduce(
                normmax[:, col:col + 1], pT[3][:], AX.X, ALU.max), inc="cv")
            set_banks([6, 7], ("cv", g.c["cv"]))

        x_ready_h1 = 0
        for Hh in range(2):
            t0h = Hh * 1024
            x_ready_cv = x_ready if Hh == 0 else x_ready_h1

            chain1 = None
            chain2 = None
            for o in range(34):
                nxt = None
                if o < 32:
                    p = o % 2
                    wait_buf("sync", ("wqk_f0h", p))
                    d0 = g.do("sync", lambda e, o=o, p=p: e.dma_start(
                        wqk_f[p][:, 0:1024], wqk_t[o, :, 0:1024]),
                        inc=f"dwq{p}")
                    wait_buf("gpsimd", ("wqk_f1h", p))
                    g.do("gpsimd", lambda e, o=o, p=p: e.dma_start(
                        wqk_f[p][:, 1024:2048], wqk_t[o, :, 1024:2048]),
                        inc=f"dwg{p}")
                if chain1 is not None:
                    emit_chain_pre(chain1)
                if o < 32:
                    g.wait("vector", f"dwq{p}", d0)
                    wait_buf("vector", ("wqk_h", p))
                    wc0 = g.do("vector", lambda e, p=p: e.tensor_copy(
                        wqk_h[p][:, 0:1024], wqk_f[p][:, 0:1024]), inc="cv")
                    set_free(("wqk_f0h", p), ("cv", wc0))
                    g.wait("scalar", f"dwg{p}")
                    wait_buf("scalar", ("wqk_h", p))
                    wcast = g.do("scalar", lambda e, p=p: e.activation(
                        wqk_h[p][:, 1024:2048], wqk_f[p][:, 1024:2048],
                        AF.Copy), inc="cs")
                    set_free(("wqk_f1h", p), ("cs", wcast))
                    set_free(("wqk_f", p), ("cv", wc0), ("cs", wcast))
                if chain2 is not None:
                    emit_chain_tail(chain2)
                    chain2 = None
                if o < 32:
                    g.wait("tensor", "cv", max(wc0, x_ready_cv))
                    mainT = pT[qk_par]
                    wait_banks("tensor", [qk_par * 2, qk_par * 2 + 1])
                    mm_done = 0
                    for d in range(DC):
                        if d == 8:
                            g.wait("tensor", "cs", wcast)
                        for tb in range(2):
                            mm_done = g.do("tensor", lambda e, p=p, d=d, tb=tb,
                                           mainT=mainT: e.matmul(
                                mainT[:, tb * 512:tb * 512 + 512],
                                wqk_h[p][:, d * 128:(d + 1) * 128],
                                x_h[:, d * 1024 + tb * 512:
                                    d * 1024 + tb * 512 + 512],
                                start=(d == 0), stop=(d == DC - 1)),
                                inc="ct" if (d == DC - 1 and tb == 1) else None)
                    set_free(("wqk_h", p), ("ct", mm_done))
                    nxt = {"o": o, "par": qk_par, "mm": mm_done, "t0h": t0h,
                           "gi": ch_i % 2}
                    ch_i += 1
                    qk_par ^= 1
                if chain1 is not None:
                    emit_chain_mid(chain1)
                    chain2 = chain1
                chain1 = nxt

            for og in range(4):
                d_mm = {}
                for d in range(DC):
                    sp = d % 4
                    q_eng = "sync" if sp % 2 == 0 else "gpsimd"
                    wait_buf(q_eng, ("wvf", sp))
                    if og == 0 and d < 4:
                        wait_buf(q_eng, ("wqk_f", 0))
                    g.do(q_eng, lambda e, og=og, d=d, sp=sp: e.dma_start(
                        wvf[sp][:], wv_r[d * 128:(d + 1) * 128,
                                         og * 512:(og + 1) * 512]),
                        inc=f"dwv{sp}")
                    g.wait("vector", f"dwv{sp}")
                    wait_buf("vector", ("wvh", sp))
                    if og == 0 and d < 4:
                        wait_buf("vector", ("wqk_h", 0))
                    wc = g.do("vector", lambda e, sp=sp: e.tensor_copy(
                        wvh[sp][:], wvf[sp][:]), inc="cv")
                    set_free(("wvf", sp), ("cv", wc))
                    g.wait("tensor", "cv", wc)
                    vmm = 0
                    for tc in range(8):
                        if d == 0:
                            wait_bank("tensor", tc)
                        vmm = g.do("tensor", lambda e, d=d, tc=tc, sp=sp:
                                   e.matmul(
                            pband[tc][:],
                            x_h[:, d * 1024 + tc * 128:d * 1024 + tc * 128 + 128],
                            wvh[sp][:],
                            start=(d == 0), stop=(d == DC - 1)),
                            inc="ct" if tc == 7 else None)
                    d_mm[d] = vmm
                    set_free(("wvh", sp), ("ct", vmm))
                    if Hh == 0 and og == 3:
                        xp = d % 2
                        if d < 2:
                            wait_buf("sync", ("wqk_f", 1))
                        wait_buf("sync", ("xs2", xp))
                        g.do("sync", lambda e, d=d, xp=xp: e.dma_start(
                            xsf2[xp][:], xT[d * 128:(d + 1) * 128, 1024:2048]),
                            inc=f"dxt{xp}")
                        g.wait("vector", f"dxt{xp}")
                        g.wait("vector", "ct", vmm)
                        xc = g.do("vector", lambda e, d=d, xp=xp: e.tensor_copy(
                            x_h[:, d * 1024:(d + 1) * 1024], xsf2[xp][:]),
                            inc="cv")
                        set_free(("xs2", xp), ("cv", xc))
                        x_ready_h1 = xc
                g.wait("scalar", "ct", d_mm[DC - 1])
                for pr in range(4):
                    col = (og * 16 + Hh * 8 + pr * 2) * 512
                    ev = g.do("scalar", lambda e, pr=pr, col=col: e.activation(
                        v_flat[:, col:col + 1024], pT[pr][:], AF.Copy),
                        inc="cs")
                    set_banks([pr * 2, pr * 2 + 1], ("cs", ev))
            if Hh == 0:
                set_free(("wqk_f", 0), ("cv", g.c["cv"]))
                set_free(("wqk_f0h", 0), ("cv", g.c["cv"]))
                set_free(("wqk_f1h", 0), ("cv", g.c["cv"]))
                set_free(("wqk_h", 0), ("ct", d_mm[DC - 1]))
                set_free(("wqk_f", 1), ("cv", x_ready_h1))
                set_free(("wqk_f0h", 1), ("cv", x_ready_h1))
                set_free(("wqk_f1h", 1), ("cv", x_ready_h1))

        g.fence("vector", "cv")
        g.do("vector", lambda e: e.tensor_tensor(
            nmax32[:], normmax[:, 0:64:2], normmax[:, 1:64:2], ALU.max),
            inc="cv")
        g.fence("vector", "cv")
        pr_cv = g.do("vector", lambda e: e.tensor_tensor(
            prod[:], nmax32[:, 0:16], nmax32[:, 16:32], ALU.mult), inc="cv")
        g.wait("scalar", "cv", pr_cv)
        g.do("scalar", lambda e: e.activation(mbias[:], prod[:], AF.Ln),
             inc="cs")
        g.fence("scalar", "cs")
        g.do("scalar", lambda e: e.activation(mbias[:], mbias[:], AF.Exp,
                                              scale=0.5, bias=cln_t[:]),
             inc="cs")
        g.wait("vector", "cs")
        mbias_done = g.do("vector", lambda e: e.tensor_scalar_mul(
            mbias[:], mbias[:], -1.0), inc="cv")

        a_end_cs = g.c["cs"]

        if DEBUG:
            g.wait("gpsimd", "cv", mbias_done)
            g.wait("gpsimd", "cs", a_end_cs)
            g.do("gpsimd", lambda e: e.dma_start(dbg_k[:], k_all[:]), inc="dg")
            g.do("gpsimd", lambda e: e.dma_start(dbg_v[:], v_flat[:]),
                 inc="dg")
            g.do("gpsimd", lambda e: e.dma_start(dbg_mb[:], mbias[:]),
                 inc="dg")
            g.do("gpsimd", lambda e: e.dma_start(dbg_r[:], nmax32[:]),
                 inc="dg")
            g.do("gpsimd", lambda e: e.dma_start(dbg_p[:], prod[:]), inc="dg")

        sb.free("x_h", "wqk_f0", "wqk_f1", "wqk_h0", "wqk_h1",
                "g16_0", "g16_1", "qs16_0", "qs16_1", "sqbuf",
                "qo0", "qo1", "normmax", "nmax32", "prod")

        if PHASES < 2:
            g.wait("gpsimd", "dg")
            g.wait("sync", "sq0", g.c["sq0"])
            g.wait("sync", "sq1", g.c["sq1"])
            g.wait("sync", "ds")
            g.wait("tensor", "ct")
            g.wait("vector", "cv")
            g.wait("scalar", "cs")
            return _finish(nc, g, block)
        q_sb = [sb.alloc(f"q_sb{i}", [128, T], F16) for i in range(2)]
        pt_b = [sb.alloc(f"pt_b{i}", [128, 1024], BF16) for i in range(3)]
        rlbc = sb.alloc("rlbc", [128, 512], F32)
        ostg = [sb.alloc(f"ostg{i}", [128, 512], F16) for i in range(2)]
        SP = [0, 1, 3]

        g.wait("tensor", "cs", a_end_cs)
        g.wait("tensor", "cv", mbias_done)
        qsp_tot = (g.c["sq0"], g.c["sq1"])
        for eng in ("vector", "scalar"):
            g.wait(eng, "sq0", qsp_tot[0])
            g.wait(eng, "sq1", qsp_tot[1])

        for h in range(NH):
            g.wait("sync", "sq0", qsp_tot[0])
            g.wait("sync", "sq1", qsp_tot[1])
            wait_buf("sync", ("q_sb", h % 2))
            qld = g.do("sync", lambda e, h=h: e.dma_start(
                q_sb[h % 2][:], qsp[h, :, :]), inc=f"dq{h % 2}")
            for qg in range(4):
                it = h * 4 + qg
                og, hs = h // 4, h % 4
                g.wait("tensor", f"dq{h % 2}", qld)
                nkt = 4 * qg + 4
                npair = nkt // 2
                av_done = 0

                def meta(kt):
                    qlo = max(kt * 128, 512 * qg)
                    nv = 512 * (qg + 1) - qlo
                    return qlo, nv, qlo - 512 * qg, kt % 2

                pr_ready = {}

                def emit_spair(pp):
                    sp = SP[pp % 3]
                    pi = pp % 3
                    wait_banks("tensor", [2 * sp, 2 * sp + 1])
                    smm = 0
                    for i2 in (0, 1):
                        kt = 2 * pp + i2
                        qlo, nv, coff, half = meta(kt)
                        smm = g.do("tensor", lambda e, h=h, kt=kt, qlo=qlo,
                                   nv=nv, half=half, sp=sp: e.matmul(
                            pT[sp][:, half * 512:half * 512 + nv],
                            k_all[:, h * T + kt * 128:h * T + kt * 128 + 128],
                            q_sb[h % 2][:, qlo:qlo + nv],
                            start=True, stop=True), inc="ct")
                    g.wait("scalar", "ct", smm)
                    wait_buf("scalar", ("pt", pi))
                    expd = g.do("scalar", lambda e, h=h, sp=sp, pi=pi:
                                e.activation(
                        pt_b[pi][:], pT[sp][:], AF.Exp,
                        bias=mbias[:, h:h + 1], scale=ISQ), inc="cs")
                    set_banks([2 * sp, 2 * sp + 1], ("cs", expd))
                    ready = ("cs", expd)
                    for i2 in (0, 1):
                        kt = 2 * pp + i2
                        if kt >= 4 * qg:
                            g.wait("vector", "cs", expd)
                            mskd = g.do("vector", lambda e, pi=pi, half=kt % 2:
                                        e.tensor_tensor(
                                pt_b[pi][:, half * 512:half * 512 + 128],
                                pt_b[pi][:, half * 512:half * 512 + 128],
                                tri_bf[:], ALU.mult), inc="cv")
                            ready = ("cv", mskd)
                    pr_ready[pp] = ready

                for pp0 in range(min(3, npair)):
                    emit_spair(pp0)
                wait_bank("tensor", 4)
                wait_bank("tensor", 5)
                for pp in range(npair):
                    g.wait("tensor", *pr_ready[pp])
                    pi = pp % 3
                    for i2 in (0, 1):
                        kt = 2 * pp + i2
                        qlo, nv, coff, half = meta(kt)
                        vcol = (og * 16 + kt) * 512 + hs * 128
                        g.do("tensor", lambda e, vcol=vcol, coff=coff, nv=nv,
                             kt=kt, half=half, nkt=nkt, pi=pi: e.matmul(
                            pband[4][:, coff:coff + nv],
                            v_flat[:, vcol:vcol + 128],
                            pt_b[pi][:, half * 512:half * 512 + nv],
                            start=(kt == 0), stop=(kt == nkt - 1)), inc="ct")
                        av_done = g.do("tensor", lambda e, coff=coff, nv=nv,
                                       kt=kt, half=half, nkt=nkt, pi=pi:
                                       e.matmul(
                            pband[5][:, coff:coff + nv],
                            ones_bf[:],
                            pt_b[pi][:, half * 512:half * 512 + nv],
                            start=(kt == 0), stop=(kt == nkt - 1)), inc="ct")
                    set_free(("pt", pi), ("ct", av_done))
                    if pp + 3 < npair:
                        emit_spair(pp + 3)
                g.wait("scalar", "ct", av_done)
                wait_buf("scalar", ("rlbc",))
                g.do("scalar", lambda e: e.activation(
                    rlbc[:], pband[5][:], AF.Ln), inc="cs")
                bank_free[5] = ("cs", g.c["cs"])
                g.fence("scalar", "cs")
                rld = g.do("scalar", lambda e: e.activation(
                    rlbc[:], rlbc[:], AF.Exp, scale=-1.0), inc="cs")
                g.wait("vector", "cs", rld)
                so = it % 2
                wait_buf("vector", ("ostg", so))
                odone = g.do("vector", lambda e, so=so:
                             e.tensor_tensor(ostg[so][:], pband[4][:], rlbc[:],
                                             ALU.mult), inc="cv")
                set_free(("rlbc",), ("cv", odone))
                bank_free[4] = ("cv", odone)
                g.wait("gpsimd", "cv", odone)
                osd = 0
                for i in range(4):
                    tci = qg * 4 + i
                    osd = g.do("gpsimd", lambda e, tci=tci, h=h, so=so, i=i:
                               e.dma_start(
                        osp[tci, :, h * 128:(h + 1) * 128],
                        ostg[so][:, i * 128:(i + 1) * 128]), inc=f"so{so}")
                set_free(("ostg", so), (f"so{so}", osd))
            set_free(("q_sb", h % 2), ("ct", g.c["ct"]))

        osp_tot = (g.c["so0"], g.c["so1"])

        sb.free("q_sb0", "q_sb1", "pt_b0", "pt_b1", "pt_b2", "rlbc",
                "ostg0", "ostg1", "k_all", "v_flat")

        if PHASES < 3:
            g.wait("gpsimd", "dg")
            g.wait("sync", "sq0", g.c["sq0"])
            g.wait("sync", "sq1", g.c["sq1"])
            g.wait("sync", "so0", g.c["so0"])
            g.wait("sync", "so1", g.c["so1"])
            g.wait("sync", "ds")
            g.wait("tensor", "ct")
            g.wait("vector", "cv")
            g.wait("scalar", "cs")
            return _finish(nc, g, block)
        b_end_ct = g.c["ct"]
        b_end_cv = g.c["cv"]
        g.wait("sync", "ct", b_end_ct)
        g.wait("sync", "cv", b_end_cv)
        for eng in ("sync", "scalar"):
            g.wait(eng, "so0", g.c["so0"])
            g.wait(eng, "so1", g.c["so1"])
            g.wait(eng, "sq0", g.c["sq0"])
            g.wait(eng, "sq1", g.c["sq1"])
        wo_h = sb.alloc("wo_h", [128, DC * 2048], F16)
        wffn = sb.alloc("wffn", [128, DC * 2048], F16)
        wof = [sb.alloc(f"wof{i}", [128, 2048], F32) for i in range(2)]
        y2 = sb.alloc("y2", [128, D], F32)
        oc = [sb.alloc(f"oc{i}", [128, 2048], F16) for i in range(2)]
        qfm = sb.alloc("qfm", [128, D], F16)
        m32 = sb.alloc("m32", [128, D], F32)
        i32a = sb.alloc("i32a", [128, D], I32)
        q32 = m32
        f32t = sb.alloc("f16t", [128, D], F16)
        idn32 = sb.alloc("idn32", [128, 128], F32)
        wsc16 = sb.alloc("wsc16", [128, D], F16)
        bia16 = sb.alloc("bia16", [128, D], F16)
        bn24 = sb.alloc("bn24", [128, 24], F32)
        m8 = sb.alloc("m8", [128, 8], F32)
        st_m = sb.alloc("st_m", [128, 1], F32)
        st_v = sb.alloc("st_v", [128, 1], F32)
        st_i = sb.alloc("st_i", [128, 1], F32)
        st_t = sb.alloc("st_t", [128, 1], F32)
        st_u = sb.alloc("st_u", [128, 1], F32)
        am = sb.alloc("am", [128, 1], F32)
        scq2 = [sb.alloc(f"scq{i}", [128, 1], F32) for i in range(2)]
        invq = sb.alloc("invq", [128, 1], F32)
        negmi = sb.alloc("negmi", [128, 1], F32)

        g.do("sync", lambda e: e.dma_start(m32[:], wsc_h[:]), inc="dwb")
        g.do("sync", lambda e: e.dma_start(y2[:], bia_h[:]), inc="dwb")
        g.do("sync", lambda e: e.dma_start(idn32[:], idn_h[:]), inc="dwb")
        g.wait("tensor", "dwb")
        g.wait("scalar", "dwb")
        g.do("scalar", lambda e: e.activation(wsc16[:], m32[:], AF.Copy),
             inc="cs")
        cbc_cs = g.do("scalar", lambda e: e.activation(bia16[:], y2[:],
                                                       AF.Copy), inc="cs")
        set_free(("y12",), ("cs", cbc_cs))
        pre_doc = {}
        for tc0 in (0, 1):
            g.do("sync", lambda e, tc0=tc0: e.dma_start(
                oc[tc0 % 2][:], osp[tc0, :, :]), inc=f"doc{tc0 % 2}")
            pre_doc[tc0] = g.c[f"doc{tc0 % 2}"]
        wo_cs = 0
        wo_cs_of = {}
        for idx in range(DC):
            p = idx % 2
            q_eng = "sync" if p == 0 else "scalar"
            wait_buf(q_eng, ("wof", p))
            g.do(q_eng, lambda e, idx=idx, p=p: e.dma_start(
                wof[p][:], wo_r[idx * 128:(idx + 1) * 128, :]), inc=f"dwo{p}")
            if p == 0:
                g.wait("scalar", f"dwo{p}")
                wo_cs = g.do("scalar", lambda e, idx=idx, p=p: e.activation(
                    wo_h[:, idx * 2048:(idx + 1) * 2048], wof[p][:], AF.Copy),
                    inc="cs")
                wo_cs_of[idx] = ("cs", wo_cs)
                set_free(("wof", p), ("cs", wo_cs))
            else:
                g.wait("vector", f"dwo{p}")
                wcv = g.do("vector", lambda e, idx=idx, p=p: e.tensor_copy(
                    wo_h[:, idx * 2048:(idx + 1) * 2048], wof[p][:]),
                    inc="cv")
                wo_cs_of[idx] = ("cv", wcv)
                set_free(("wof", p), ("cv", wcv))
        for kd in range(DC):
            g.do("sync" if kd % 2 == 0 else "scalar",
                 lambda e, kd=kd: e.dma_start(
                wffn[:, kd * 2048:(kd + 1) * 2048],
                wffn_r[kd * 128:(kd + 1) * 128, :]), inc="dff")
        wld_ff = g.c["dff"]

        sb.free("wof0", "wof1")
        xtm = [sb.alloc(f"xtm{i}", [128, D], F32) for i in range(2)]
        set_free(("xtm", 0), wo_cs_of[DC - 2], wo_cs_of[DC - 1])
        set_free(("xtm", 1), wo_cs_of[DC - 2], wo_cs_of[DC - 1])

        def ln_stats(src_h):
            g.fence("vector", "cv")
            for j in range(4):
                g.do("vector", lambda e, j=j: e.bn_stats(
                    bn24[:, j * 6:(j + 1) * 6],
                    src_h[:, j * 512:(j + 1) * 512]), inc="cv")
            g.fence("vector", "cv")
            g.do("vector", lambda e: e.tensor_tensor(
                m8[:], bn24[:, 1:24:3], bn24[:, 1:24:3], ALU.mult), inc="cv")
            g.do("vector", lambda e: e.tensor_reduce(
                st_m[:], bn24[:, 1:24:3], AX.X, ALU.add), inc="cv")
            g.fence("vector", "cv")
            g.do("vector", lambda e: e.tensor_scalar_mul(st_m[:], st_m[:],
                                                         1.0 / 8.0), inc="cv")
            g.do("vector", lambda e: e.tensor_reduce(
                st_t[:], m8[:], AX.X, ALU.add), inc="cv")
            g.do("vector", lambda e: e.tensor_reduce(
                st_v[:], bn24[:, 2:24:3], AX.X, ALU.add), inc="cv")
            g.fence("vector", "cv")
            g.do("vector", lambda e: e.scalar_tensor_tensor(
                st_v[:], st_v[:], 1.0 / D, st_t[:], ALU.mult, ALU.bypass),
                 inc="cv")
            g.fence("vector", "cv")
            g.do("vector", lambda e: e.scalar_tensor_tensor(
                st_v[:], st_t[:], 1.0 / 8.0, st_v[:], ALU.mult, ALU.add),
                 inc="cv")
            g.do("vector", lambda e: e.tensor_tensor(
                st_u[:], st_m[:], st_m[:], ALU.mult), inc="cv")
            g.fence("vector", "cv")
            vstat = g.do("vector", lambda e: e.tensor_tensor(
                st_v[:], st_v[:], st_u[:], ALU.subtract), inc="cv")
            g.wait("scalar", "cv", vstat)
            g.fence("scalar", "cs")
            g.do("scalar", lambda e: e.activation(st_i[:], st_v[:], AF.Ln,
                                                  bias=eps_t[:]), inc="cs")
            g.fence("scalar", "cs")
            g.do("scalar", lambda e: e.activation(st_i[:], st_i[:], AF.Exp,
                                                  scale=-0.5), inc="cs")
            g.fence("scalar", "cs")
            g.wait("vector", "cs")
            g.fence("vector", "cv")

        qd_of = {}
        womm_of = {}
        std_of = {}

        def emit_loads(tc):
            pc = tc % 2
            g.wait("sync", "so0", osp_tot[0])
            g.wait("sync", "so1", osp_tot[1])
            wait_buf("sync", ("y12",))
            if tc in pre_doc:
                docc = pre_doc[tc]
            else:
                wait_buf("sync", ("oc", pc))
                g.do("sync", lambda e, tc=tc, pc=pc: e.dma_start(
                    oc[pc][:], osp[tc, :, :]), inc=f"doc{pc}")
                docc = g.c[f"doc{pc}"]
            g.wait("gpsimd", "so0", osp_tot[0])
            g.wait("gpsimd", "so1", osp_tot[1])
            wait_buf("gpsimd", ("y12",))
            wait_buf("gpsimd", ("xtm", pc))
            g.do("gpsimd", lambda e, tc=tc, pc=pc: e.dma_start(
                xtm[pc][:], x_tm[tc * 128:(tc + 1) * 128, :]), inc=f"dxg{pc}")
            return (g.c[f"dxg{pc}"], docc)

        def emit_womm(tc, cld):
            pc = tc % 2
            g.wait("tensor", f"doc{pc}", cld[1])
            for gg in range(4):
                wait_bank("tensor", gg)
            womm = 0
            for idx in range(DC):
                g.wait("tensor", *wo_cs_of[idx])
                for gg in range(4):
                    womm = g.do("tensor", lambda e, idx=idx, gg=gg, pc=pc:
                                e.matmul(
                        pband[gg][:],
                        oc[pc][:, idx * 128:(idx + 1) * 128],
                        wo_h[:, idx * 2048 + gg * 512:idx * 2048 + gg * 512 + 512],
                        start=(idx == 0), stop=(idx == DC - 1)),
                        inc="ct" if idx == DC - 1 else None)
            set_free(("oc", pc), ("ct", womm))
            womm_of[tc] = womm

        def emit_evac_wo(tc, cld):
            pc = tc % 2
            g.wait("vector", "ct", womm_of[tc])
            g.wait("vector", f"dxg{pc}", cld[0])
            for gg in range(4):
                ev = g.do("vector", lambda e, gg=gg, pc=pc: e.tensor_tensor(
                    xtm[pc][:, gg * 512:(gg + 1) * 512], pband[gg][:],
                    xtm[pc][:, gg * 512:(gg + 1) * 512], ALU.add), inc="cv")
                bank_free[gg] = ("cv", ev)

        def emit_evac_ln_quant(tc):
            pc = tc % 2
            yab = xtm[pc]
            ln_stats(yab)
            g.do("vector", lambda e, yab=yab: e.tensor_scalar(
                yab[:], yab[:], st_m[:], st_i[:], ALU.subtract, ALU.mult),
                inc="cv")
            g.fence("vector", "cv")
            g.do("vector", lambda e, yab=yab: e.tensor_reduce(
                am[:], yab[:], AX.X, ALU.max, apply_absolute_value=True),
                inc="cv")
            g.fence("vector", "cv")
            g.do("vector", lambda e: e.tensor_scalar_max(am[:], am[:], EPS_Q),
                 inc="cv")
            g.fence("vector", "cv")
            g.do("vector", lambda e, tc=tc: e.tensor_scalar_mul(
                scq2[tc % 2][:], am[:], 1.0 / 127.0), inc="cv")
            g.do("vector", lambda e: e.reciprocal(invq[:], am[:]), inc="cv")
            g.fence("vector", "cv")
            g.do("vector", lambda e: e.tensor_scalar_mul(
                invq[:], invq[:], 127.0), inc="cv")
            g.fence("vector", "cv")
            if tc - 1 in trd_of:
                g.wait("vector", "ct", trd_of[tc - 1])
            tq = g.do("vector", lambda e, yab=yab: e.tensor_scalar(
                m32[:], yab[:], invq[:], 256.0, ALU.mult, ALU.add), inc="cv")
            g.fence("vector", "cv")
            g.do("vector", lambda e: e.tensor_copy(i32a[:], m32[:]), inc="cv")
            g.fence("vector", "cv")
            g.do("vector", lambda e: e.tensor_tensor(
                m32[:], m32[:], i32a[:], ALU.subtract), inc="cv")
            g.fence("vector", "cv")
            qd_of[tc] = g.do("vector", lambda e: e.scalar_tensor_tensor(
                q32[:], m32[:], 0.5, i32a[:], ALU.is_ge, ALU.add), inc="cv")

        trd_of = {}

        fmm_of = {}

        def emit_post_tensor(tc):
            g.wait("tensor", "cv", qd_of[tc])
            wait_banks("tensor", [4, 5, 6, 7])
            trd0 = 0
            for kd in range(16):
                if kd == 8:
                    trd0 = g.c["ct"]
                trd = g.do("tensor", lambda e, kd=kd: e.transpose(
                    pT[2 + kd // 8][:, (kd % 8) * 128:(kd % 8) * 128 + 128],
                    q32[:, kd * 128:(kd + 1) * 128], idn32[:]), inc="ct")
            trd_of[tc] = trd
            g.wait("scalar", "ct", trd0)
            qs0 = g.do("scalar", lambda e: e.activation(
                qfm[:, 0:1024], pT[2][:], AF.Copy, bias=-256.0), inc="cs")
            set_banks([4, 5], ("cs", qs0))
            g.wait("scalar", "ct", trd)
            qs1 = g.do("scalar", lambda e: e.activation(
                qfm[:, 1024:2048], pT[3][:], AF.Copy, bias=-256.0), inc="cs")
            set_banks([6, 7], ("cs", qs1))
            g.wait("tensor", "dff", wld_ff)
            for rr in range(2):
                bb = 4 + rr * 2
                for gg in range(2):
                    wait_bank("tensor", bb + gg)
                fmm = 0
                for kd in range(DC):
                    if kd == 0 and rr == 0:
                        g.wait("tensor", "cs", qs0)
                    if kd == 8 and rr == 0:
                        g.wait("tensor", "cs", qs1)
                    for gg in range(2):
                        ng = rr * 2 + gg
                        fmm = g.do("tensor", lambda e, kd=kd, gg=gg, ng=ng,
                                   bb=bb: e.matmul(
                            pband[bb + gg][:],
                            qfm[:, kd * 128:(kd + 1) * 128],
                            wffn[:, kd * 2048 + ng * 512:kd * 2048 + ng * 512 + 512],
                            start=(kd == 0), stop=(kd == DC - 1)),
                            inc="ct" if kd == DC - 1 else None)
                fmm_of[(tc, rr)] = fmm

        def emit_post_vector(tc):
            if tc - 1 in std_of:
                g.wait("vector", "dg", std_of[tc - 1])
            sq2 = scq2[tc % 2]
            for rr in range(2):
                bb = 4 + rr * 2
                g.wait("vector", "ct", fmm_of[(tc, rr)])
                for gg in range(2):
                    ng = rr * 2 + gg
                    g.do("vector", lambda e, gg=gg, ng=ng, bb=bb:
                         e.tensor_tensor(
                        y2[:, ng * 512:(ng + 1) * 512], pband[bb + gg][:],
                        wsc16[:, ng * 512:(ng + 1) * 512], ALU.mult), inc="cv")
                g.fence("vector", "cv")
                for gg in range(2):
                    ng = rr * 2 + gg
                    fev = g.do("vector", lambda e, gg=gg, ng=ng, sq2=sq2:
                               e.scalar_tensor_tensor(
                        f32t[:, ng * 512:(ng + 1) * 512],
                        y2[:, ng * 512:(ng + 1) * 512], sq2[:],
                        bia16[:, ng * 512:(ng + 1) * 512],
                        ALU.mult, ALU.add), inc="cv")
                    bank_free[bb + gg] = ("cv", fev)
            g.fence("vector", "cv")
            yprev = xtm[tc % 2]
            ya_done = g.do("vector", lambda e, yprev=yprev: e.tensor_tensor(
                y2[:], yprev[:], f32t[:], ALU.add), inc="cv")
            set_free(("xtm", tc % 2), ("cv", ya_done))
            ln_stats(y2)
            y2d = g.do("vector", lambda e: e.tensor_scalar(
                y2[:], y2[:], st_m[:], st_i[:], ALU.subtract, ALU.mult),
                inc="cv")
            g.wait("gpsimd", "cv", y2d)
            std_of[tc] = g.do("gpsimd", lambda e, tc=tc: e.dma_start(
                out[tc * 128:(tc + 1) * 128, :], y2[:]), inc="dg")

        for it in range(18):
            if it >= 2:
                emit_post_tensor(it - 2)
            if 0 <= it - 1 < 16:
                emit_evac_ln_quant(it - 1)
            if it >= 2:
                emit_post_vector(it - 2)
            if it < 16:
                cld = emit_loads(it)
                emit_womm(it, cld)
                emit_evac_wo(it, cld)

        g.wait("gpsimd", "dg")
        g.wait("sync", "ds")
        g.wait("sync", "dg")
        g.wait("sync", "sq0", g.c["sq0"])
        g.wait("sync", "sq1", g.c["sq1"])
        g.wait("sync", "so0", g.c["so0"])
        g.wait("sync", "so1", g.c["so1"])
        return _finish(nc, g, block)


def _finish(nc, g, block):
    @block.sync
    def _(e):
        g.run("sync", e)

    @block.gpsimd
    def _(e):
        g.run("gpsimd", e)

    @block.scalar
    def _(e):
        g.run("scalar", e)

    @block.vector
    def _(e):
        g.run("vector", e)

    @block.tensor
    def _(e):
        g.run("tensor", e)

    return nc


def _host_prep(L0_attn_x, L0_attn_Wqkv, L0_attn_WO, L0_ffn_weight,
               L0_ffn_w_scale, L0_ffn_bias):
    x = np.ascontiguousarray(np.asarray(L0_attn_x, dtype=np.float32))
    Wqkv = np.ascontiguousarray(np.asarray(L0_attn_Wqkv, dtype=np.float32))
    WO = np.ascontiguousarray(np.asarray(L0_attn_WO, dtype=np.float32))
    wffn = np.asarray(L0_ffn_weight)
    wsc = np.asarray(L0_ffn_w_scale, dtype=np.float32)
    bia = np.asarray(L0_ffn_bias, dtype=np.float32)

    Wqk = Wqkv[:2 * D].reshape(32, 128, DC, 128).transpose(0, 3, 2, 1)
    wqk_t = np.ascontiguousarray(Wqk).reshape(32, 128, 2048)
    wv_r = np.ascontiguousarray(Wqkv[2 * D:].T)
    wo_r = np.ascontiguousarray(WO.T)
    wffn_r = np.ascontiguousarray(wffn.T).astype(np.float16)

    pos = np.arange(T, dtype=np.float32)
    freqs = np.exp((-math.log(10000.0) / DH) *
                   np.arange(0, DH, 2, dtype=np.float32)).astype(np.float32)
    ang = (freqs[:, None] * pos[None, :]).astype(np.float32)
    c64 = np.cos(ang.astype(np.float64)).astype(np.float32)
    s64 = np.sin(ang.astype(np.float64)).astype(np.float32)
    ropec = np.repeat(c64, 2, axis=0)
    ropes_ = np.empty((128, T), np.float32)
    ropes_[0::2] = -s64
    ropes_[1::2] = s64

    ii, jj = np.meshgrid(np.arange(128), np.arange(128), indexing="ij")
    tri = (ii <= jj).astype(np.float32)
    rsw = ((ii ^ 1) == jj).astype(np.float32)
    idn = np.eye(128, dtype=np.float32)

    wsc_bc = np.ascontiguousarray(
        np.broadcast_to(wsc[:, 0][None, :], (128, D))).astype(np.float32)
    bia_bc = np.ascontiguousarray(
        np.broadcast_to(bia[None, :], (128, D))).astype(np.float32)

    shared = dict(wqk_t=wqk_t, wv_r=wv_r, wo_r=wo_r, wffn_r=wffn_r,
                  ropec=ropec, ropes=ropes_, tri_h=tri, rsw_h=rsw, idn_h=idn,
                  wsc_h=wsc_bc, bia_h=bia_bc)
    in_maps = []
    for b in range(NCORES):
        m = dict(shared)
        m["xT"] = np.ascontiguousarray(x[b].T)
        m["x_tm"] = x[b]
        in_maps.append(m)
    return in_maps


def kernel(L0_attn_x, L0_attn_Wqkv, L0_attn_WO, L0_ffn_weight,
           L0_ffn_w_scale, L0_ffn_bias):
    in_maps = _host_prep(L0_attn_x, L0_attn_Wqkv, L0_attn_WO, L0_ffn_weight,
                         L0_ffn_w_scale, L0_ffn_bias)
    nc = build_program()
    res = run_bass_kernel_spmd(nc, in_maps, core_ids=list(range(NCORES)))
    return np.stack([np.asarray(res.results[b]["out"], dtype=np.float32)
                     for b in range(NCORES)], axis=0)
